# version 07
# baseline (speedup 1.0000x reference)
import numpy as np
from bisect import bisect_left
from contextlib import ExitStack
import concourse.bass as bass
import concourse.mybir as mybir
from concourse.bass_utils import run_bass_kernel_spmd

F32 = mybir.dt.float32
BF16 = mybir.dt.bfloat16
AF = mybir.ActivationFunctionType
ALU = mybir.AluOpType
AX = mybir.AxisListType

EXP_SHIFT = -20.0


class CFG:
    def __init__(self, D=2048, F=4096, S=2048, B=4, M=4):
        self.D, self.F, self.S, self.B, self.M = D, F, S, B, M
        self.DC = D // 128
        self.FC = F // 128
        self.NB = S // 128
        self.NOB = self.NB // 2
        self.T = self.NOB * 128
        self.TC = min(512, self.T)
        self.NTC = self.T // self.TC
        self.LC = min(256, self.T)
        self.NLC = self.T // self.LC


def own_blocks(sub, NB):
    return [g for g in range(NB) if ((g % 4) in (0, 3)) == (sub == 0)]


class Ctr:
    def __init__(self, es, nc, name):
        self.h = es.enter_context(nc.semaphore(name))
        self.v = 0
        self.name = name


def build(cfg):
    nc = bass.Bass("TRN2", target_bir_lowering=False, debug=False)
    D, F, T, DC, FC, M = cfg.D, cfg.F, cfg.T, cfg.DC, cfg.FC, cfg.M
    NB, NOB, TC, NTC = cfg.NB, cfg.NOB, cfg.TC, cfg.NTC
    LC, NLC = cfg.LC, cfg.NLC
    OB0, OB1 = own_blocks(0, NB), own_blocks(1, NB)
    ustart_blk = [min(bisect_left(OB0, j), bisect_left(OB1, j)) for j in range(NB)]
    owner_of = {}
    pos_of = {}
    for s_, ob in ((0, OB0), (1, OB1)):
        for p_, g in enumerate(ob):
            owner_of[g] = s_
            pos_of[g] = p_

    dr = lambda n, s, d, k="ExternalInput": nc.dram_tensor(n, s, d, kind=k)
    x_h = dr("x", [D, T], F32)
    out_h = dr("out", [D, T], F32, "ExternalOutput")
    wqT = dr("wqT", [D, D], F32)
    wkT = dr("wkT", [D, D], F32)
    wvT = dr("wvT", [D, D], F32)
    woT = dr("woT", [D, D], F32)
    W0aT_h = dr("W0aT", [D, F], F32)
    W0bT_h = dr("W0bT", [F, D], F32)
    W1aT_h = dr("W1aT", [D, F], F32)
    W1bT_h = dr("W1bT", [F, D], F32)
    WgT_h = dr("WgT", [D, 2], F32)
    mems_h = dr("mems", [M, D, D], F32)
    normsT_h = dr("normsT", [D, M], F32)
    b0a_h = dr("b0a2", [128, FC], F32)
    b1a_h = dr("b1a2", [128, FC], F32)
    b0b_h = dr("b0b2", [128, DC], F32)
    b1b_h = dr("b1b2", [128, DC], F32)
    gate_h = dr("gate", [1, 1], F32)
    amask_h = dr("amask", [128, NB, 256], BF16)
    xsp = dr("xsp", [D, T], F32, "Internal")
    kin = dr("kin", [D, T], BF16, "Internal")
    vin = dr("vin", [T, D], BF16, "Internal")
    kg = dr("kg", [2 * D, T], BF16, "Internal")
    vg = dr("vg", [2 * T, D], BF16, "Internal")
    smin = dr("smin", [1, 8], F32, "Internal")
    smg = dr("smg", [2, 8], F32, "Internal")

    r3 = lambda h: h.ap().rearrange("(c p) n -> p c n", p=128)
    x_r = r3(x_h); out_r = r3(out_h); xsp_r = r3(xsp)
    kin_r = r3(kin); vin_r = r3(vin)
    wq_r = r3(wqT); wk_r = r3(wkT); wv_r = r3(wvT); wo_r = r3(woT)
    W0a_r = r3(W0aT_h); W1a_r = r3(W1aT_h)
    W0b_r = r3(W0bT_h); W1b_r = r3(W1bT_h)
    wg_r = r3(WgT_h); norms_r = r3(normsT_h)
    mems_r = mems_h.ap().rearrange("m (c p) e -> m p c e", p=128)
    kg_r = r3(kg)
    vg_r = r3(vg)
    amask_r = amask_h.ap()

    waited = {}
    guards = {}
    floor = []

    def W(eng, engname, ctr, val):
        if val <= 0:
            return
        k = (engname, ctr.name)
        if waited.get(k, 0) >= val:
            return
        eng.wait_ge(ctr.h, val)
        waited[k] = val

    def gwait(eng, engname, key):
        g = guards.get(key)
        if g is not None:
            W(eng, engname, g[0], g[1])

    def gset(key, ctr, val):
        guards[key] = (ctr, val)

    ACOLS = max(DC * 256, FC * 128, 16)

    es0 = ExitStack()
    with es0:
        sb = lambda n, s, d: es0.enter_context(nc.sbuf_tensor(n, s, d))
        ones1 = sb("ones1", [1, 128], F32)
        ones1b = sb("ones1b", [1, 128], BF16)
        onescol = sb("onescol", [128, 1], BF16)
        b0a_t = sb("b0a_t", [128, FC], F32)
        b1a_t = sb("b1a_t", [128, FC], F32)
        b0b_t = sb("b0b_t", [128, DC], F32)
        b1b_t = sb("b1b_t", [128, DC], F32)
        wg_t = sb("wg_t", [128, DC, 2], BF16)
        norms_t = sb("norms_t", [128, DC, M], BF16)
        gate_t = sb("gate_t", [1, 1], F32)
        gsig = sb("gsig", [1, 1], F32)
        omg = sb("omg", [1, 1], F32)
        srow = sb("srow", [1, 16], F32)
        srow2 = sb("srow2", [1, 16], F32)
        wa = [sb("wa0", [128, ACOLS], BF16), sb("wa1", [128, ACOLS], BF16)]
        scrf = sb("scrf", [128, 1024], F32)
        smin_t = sb("smin_t", [1, 8], F32)
        eps_t = sb("eps_t", [1, 1], F32)
        shift_t = sb("shift_t", [128, 1], F32)
        smg_t0 = sb("smg_t0", [1, 8], F32)
        smg_t1 = sb("smg_t1", [1, 8], F32)
        amask_t = sb("amask_t", [128, NB, 256], BF16)

        SCP = es0.enter_context(nc.psum_tensor("scp", [128, 4, 512], F32))
        PS = [SCP[:, i, :] for i in range(4)] + [
            es0.enter_context(nc.psum_tensor(f"ps{i}", [128, 512], F32))
            for i in range(4, 8)]

        hd = Ctr(es0, nc, "hd")
        pd = Ctr(es0, nc, "pd")
        pw = [Ctr(es0, nc, "pw0"), Ctr(es0, nc, "pw1")]
        hsp = Ctr(es0, nc, "hsp")
        hk = Ctr(es0, nc, "hk")
        hv = Ctr(es0, nc, "hv")
        hsn = Ctr(es0, nc, "hsn")
        hsm2 = Ctr(es0, nc, "hsm2")
        hsl = Ctr(es0, nc, "hsl")
        hx2 = [Ctr(es0, nc, "hx2a"), Ctr(es0, nc, "hx2b")]
        hout = Ctr(es0, nc, "hout")
        pe = Ctr(es0, nc, "pe")
        ac = Ctr(es0, nc, "ac")
        dv = Ctr(es0, nc, "dv")
        pl = Ctr(es0, nc, "pl")
        cc1 = Ctr(es0, nc, "cc1")
        cc2 = Ctr(es0, nc, "cc2")

        all_ctrs = [hd, pd, pe, ac, dv, pl, hsp, hk, hv, hsn, hsm2, hsl, hout] + pw + hx2
        block = es0.enter_context(nc.Block())
        disp = {'pe': block.tensor, 'ac': block.scalar, 'dv': block.vector,
                'pl': block.gpsimd, 'sp': block.sync}

        def on(engname, f):
            def g(eng):
                for (c, v) in floor:
                    W(eng, engname, c, v)
                f(eng)
            disp[engname](g)

        def close_scope(es):
            es.close()
            for c in all_ctrs:
                floor.append((c, c.v))

        esU1 = ExitStack()
        u_t = esU1.enter_context(nc.sbuf_tensor("ua", [128, DC, T], BF16))
        esX = ExitStack()
        X = esX.enter_context(nc.sbuf_tensor("X", [128, DC, T], F32))

        def p0(sp):
            q4 = DC // 4
            for qi in range(4):
                sp.dma_start(out=X[:, qi * q4:(qi + 1) * q4, :],
                             in_=x_r[:, qi * q4:(qi + 1) * q4, :]).then_inc(hd.h, 16)
                hd.v += 16
            for t, src in ((b0a_t[:], b0a_h.ap()), (b1a_t[:], b1a_h.ap()),
                           (b0b_t[:], b0b_h.ap()), (b1b_t[:], b1b_h.ap()),
                           (gate_t[:], gate_h.ap()), (amask_t[:], amask_r)):
                sp.dma_start(out=t, in_=src).then_inc(hd.h, 16)
                hd.v += 16
        on('sp', p0)
        consts_hd = hd.v
        x_ready = consts_hd

        def p0b(gp):
            gp.dma_start(out=wg_t[:], in_=wg_r).then_inc(pd.h, 16)
            gp.dma_start(out=norms_t[:], in_=norms_r).then_inc(pd.h, 16)
            pd.v += 32
            gp.memset(ones1[:], 1.0).then_inc(pl.h, 1)
            gp.memset(ones1b[:], 1.0).then_inc(pl.h, 1)
            pl.v += 1
            gp.memset(onescol[:], 1.0).then_inc(pl.h, 1)
            gp.memset(eps_t[:], 1e-5).then_inc(pl.h, 1)
            gp.memset(shift_t[:], EXP_SHIFT).then_inc(pl.h, 1)
            pl.v += 4
        on('pl', p0b)
        consts_pd = pd.v
        consts_pl = pl.v

        ln_i = [0]

        def emit_ln(Xsrc, masks, x_req, u_dst):
            ln_i[0] += 1
            esL = ExitStack()
            scr_b = esL.enter_context(nc.sbuf_tensor(f"lnb{ln_i[0]}", [128, 2 * DC * LC], BF16))
            rows = esL.enter_context(nc.sbuf_tensor(f"lnr{ln_i[0]}", [1, 8 * LC], F32))
            inv_d = 1.0 / D
            xqoff = DC * LC
            for t in range(NLC):
                cs, ce = t * LC, (t + 1) * LC

                def cast_body(sc, cs=cs, ce=ce):
                    W(sc, 'ac', *x_req)
                    gwait(sc, 'ac', 'lnscr')
                    for c in range(DC):
                        sc.activation(out=scr_b[:, c * LC:(c + 1) * LC],
                                      in_=Xsrc[:, c, cs:ce], func=AF.Copy)
                        ins = sc.activation(out=scr_b[:, xqoff + c * LC: xqoff + (c + 1) * LC],
                                            in_=Xsrc[:, c, cs:ce], func=AF.Square)
                    ins.then_inc(ac.h, 1); ac.v += 1
                on('ac', cast_body)
                cast_v = ac.v

                def stat_mm(te):
                    W(te, 'pe', ac, cast_v)
                    W(te, 'pe', pl, consts_pl)
                    gwait(te, 'pe', ('ps', 4))
                    gwait(te, 'pe', ('ps', 5))
                    for c in range(DC):
                        te.matmul(PS[4][0:1, :LC], onescol[:],
                                  scr_b[:, c * LC:(c + 1) * LC],
                                  start=(c == 0), stop=(c == DC - 1))
                    for c in range(DC):
                        ins = te.matmul(PS[5][0:1, :LC], onescol[:],
                                        scr_b[:, xqoff + c * LC: xqoff + (c + 1) * LC],
                                        start=(c == 0), stop=(c == DC - 1))
                    ins.then_inc(pe.h, 1); pe.v += 1
                on('pe', stat_mm)
                mm_v = pe.v
                gset('lnscr', pe, mm_v)

                mean_r = rows[0:1, 0:LC]
                var_r = rows[0:1, LC:2 * LC]
                msq_r = rows[0:1, 2 * LC:3 * LC]
                rstd_r = rows[0:1, 3 * LC:4 * LC]

                def stats(ve):
                    W(ve, 'dv', pe, mm_v)
                    gwait(ve, 'dv', 'lnrows')
                    ve.tensor_scalar_mul(mean_r, PS[4][0:1, :LC], inv_d).then_inc(dv.h, 1)
                    dv.v += 1
                    W(ve, 'dv', dv, dv.v)
                    ve.tensor_mul(msq_r, mean_r, mean_r).then_inc(dv.h, 1)
                    dv.v += 1
                    W(ve, 'dv', dv, dv.v)
                    ve.scalar_tensor_tensor(var_r, PS[5][0:1, :LC], inv_d,
                                            msq_r, ALU.mult, ALU.subtract).then_inc(dv.h, 1)
                    dv.v += 1
                on('dv', stats)
                gset(('ps', 4), dv, dv.v)
                gset(('ps', 5), dv, dv.v)
                var_v = dv.v

                def rstd1(sc):
                    W(sc, 'ac', dv, var_v)
                    W(sc, 'ac', pl, consts_pl)
                    sc.activation(out=msq_r, in_=var_r, func=AF.Sqrt,
                                  bias=eps_t[0:1, 0:1]).then_inc(ac.h, 1)
                    ac.v += 1
                on('ac', rstd1)
                sq_v = ac.v

                def rstd2(ve):
                    W(ve, 'dv', ac, sq_v)
                    ve.reciprocal(rstd_r, msq_r).then_inc(dv.h, 1)
                    dv.v += 1
                on('dv', rstd2)
                rstd_v = dv.v

                def bcast(te):
                    W(te, 'pe', dv, rstd_v)
                    gwait(te, 'pe', ('ps', 6))
                    gwait(te, 'pe', ('ps', 7))
                    te.matmul(PS[6][:, :LC], ones1[:], mean_r,
                              start=True, stop=True).then_inc(pe.h, 1)
                    te.matmul(PS[7][:, :LC], ones1[:], rstd_r,
                              start=True, stop=True).then_inc(pe.h, 1)
                    pe.v += 2
                on('pe', bcast)
                bc_v = pe.v

                def ubuild(ve, cs=cs, ce=ce):
                    W(ve, 'dv', pe, bc_v)
                    gwait(ve, 'dv', 'uarena')
                    for c in range(DC):
                        ve.tensor_sub(scrf[:, 0:LC], Xsrc[:, c, cs:ce],
                                      PS[6][:, :LC]).then_inc(dv.h, 1)
                        dv.v += 1
                        W(ve, 'dv', dv, dv.v)
                        ve.tensor_mul(u_dst[:, c, cs:ce], scrf[:, 0:LC],
                                      PS[7][:, :LC]).then_inc(dv.h, 1)
                        dv.v += 1
                        W(ve, 'dv', dv, dv.v)
                on('dv', ubuild)
                gset(('ps', 6), dv, dv.v)
                gset(('ps', 7), dv, dv.v)
                gset('lnrows', dv, dv.v)
                u_v = dv.v

                if masks is not None:
                    m0bf, m1bf, m0f, m1f = masks

                    def logit_mm(te, cs=cs, ce=ce):
                        W(te, 'pe', dv, u_v)
                        W(te, 'pe', pd, consts_pd)
                        gwait(te, 'pe', ('ps', 4))
                        gwait(te, 'pe', ('ps', 5))
                        for c in range(DC):
                            te.matmul(PS[4][0:1, :LC], wg_t[:, c, 0:1],
                                      u_dst[:, c, cs:ce], start=(c == 0), stop=(c == DC - 1))
                        for c in range(DC):
                            ins = te.matmul(PS[5][0:1, :LC], wg_t[:, c, 1:2],
                                            u_dst[:, c, cs:ce], start=(c == 0), stop=(c == DC - 1))
                        ins.then_inc(pe.h, 1); pe.v += 1
                    on('pe', logit_mm)
                    lg_v = pe.v
                    l1_r = rows[0:1, 4 * LC:5 * LC]
                    d_r = rows[0:1, 5 * LC:6 * LC]
                    mm0_r = rows[0:1, 6 * LC:7 * LC]
                    mm1_r = rows[0:1, 7 * LC:8 * LC]

                    def maskrows(ve):
                        W(ve, 'dv', pe, lg_v)
                        gwait(ve, 'dv', 'lnmrows')
                        ve.tensor_copy(l1_r, PS[5][0:1, :LC]).then_inc(dv.h, 1)
                        dv.v += 1
                        W(ve, 'dv', dv, dv.v)
                        ve.tensor_sub(d_r, PS[4][0:1, :LC], l1_r).then_inc(dv.h, 1)
                        dv.v += 1
                        W(ve, 'dv', dv, dv.v)
                        ve.tensor_single_scalar(mm0_r, d_r, 0.0, ALU.is_ge).then_inc(dv.h, 1)
                        dv.v += 1
                        W(ve, 'dv', dv, dv.v)
                        ve.tensor_scalar(mm1_r, mm0_r, -0.5, 0.5,
                                         ALU.mult, ALU.add).then_inc(dv.h, 1)
                        dv.v += 1
                        W(ve, 'dv', dv, dv.v)
                        ve.tensor_scalar_mul(mm0_r, mm0_r, 0.5).then_inc(dv.h, 1)
                        dv.v += 1
                    on('dv', maskrows)
                    gset(('ps', 4), dv, dv.v)
                    gset(('ps', 5), dv, dv.v)
                    mr_v = dv.v

                    def mask_bcast(te):
                        W(te, 'pe', dv, mr_v)
                        gwait(te, 'pe', ('ps', 6))
                        gwait(te, 'pe', ('ps', 7))
                        te.matmul(PS[6][:, :LC], ones1[:], mm0_r,
                                  start=True, stop=True).then_inc(pe.h, 1)
                        te.matmul(PS[7][:, :LC], ones1[:], mm1_r,
                                  start=True, stop=True).then_inc(pe.h, 1)
                        pe.v += 2
                    on('pe', mask_bcast)
                    mb_v = pe.v
                    gset('lnmrows', pe, mb_v)

                    def mask_copy(ve, cs=cs, ce=ce):
                        W(ve, 'dv', pe, mb_v)
                        ve.tensor_copy(m0f[:, cs:ce], PS[6][:, :LC]).then_inc(dv.h, 1)
                        ve.tensor_copy(m0bf[:, cs:ce], PS[6][:, :LC]).then_inc(dv.h, 1)
                        ve.tensor_copy(m1f[:, cs:ce], PS[7][:, :LC]).then_inc(dv.h, 1)
                        ve.tensor_copy(m1bf[:, cs:ce], PS[7][:, :LC]).then_inc(dv.h, 1)
                        dv.v += 4
                    on('dv', mask_copy)
                    gset(('ps', 6), dv, dv.v)
                    gset(('ps', 7), dv, dv.v)
            gset('lnscr', dv, dv.v)
            u_done = dv.v
            close_scope(esL)
            return u_done

        def emit_ffn(Xdst, u_ready, masks, tag, u_src):
            m0bf, m1bf, m0f, m1f = masks
            esF = ExitStack()
            h_sb = esF.enter_context(nc.sbuf_tensor(f"h{tag}", [128, FC * T], BF16))
            scr2 = esF.enter_context(nc.sbuf_tensor(f"s2{tag}", [128, 512], F32))
            SUPA = 256
            NSA = F // SUPA
            xadd = {}
            for ei, (Wa_r, Wb_r, ba, bb, fn, mbf, mf) in enumerate((
                    (W0a_r, W0b_r, b0a_t, b0b_t, AF.Gelu, m0bf, m0f),
                    (W1a_r, W1b_r, b1a_t, b1b_t, AF.Relu, m1bf, m1f))):
                loadv = {}

                def emit_load_a(sj, Wa_r=Wa_r, loadv=loadv):
                    def body(gp, sj=sj):
                        gwait(gp, 'pl', ('arena', sj % 2))
                        w_ = pw[sj % 2]
                        hc = DC // 2
                        gp.dma_start(out=wa[sj % 2][:, :hc * SUPA],
                                     in_=Wa_r[:, :hc, sj * SUPA:(sj + 1) * SUPA]
                                     ).then_inc(w_.h, 16)
                        gp.dma_start(out=wa[sj % 2][:, hc * SUPA:DC * SUPA],
                                     in_=Wa_r[:, hc:, sj * SUPA:(sj + 1) * SUPA]
                                     ).then_inc(w_.h, 16)
                        w_.v += 32
                        loadv[sj] = (w_, w_.v)
                    on('pl', body)
                emit_load_a(0)
                if NSA > 1:
                    emit_load_a(1)
                for sj in range(NSA):
                    last_mm = 0
                    for fo in range(SUPA // 128):
                        f = sj * (SUPA // 128) + fo
                        for tc in range(NTC):
                            cs, ce = tc * TC, (tc + 1) * TC
                            slot = (f * NTC + tc) % 2

                            def mm_body(te, sj=sj, fo=fo, cs=cs, ce=ce, slot=slot):
                                W(te, 'pe', *loadv[sj])
                                W(te, 'pe', dv, u_ready)
                                gwait(te, 'pe', ('ps', slot))
                                for c in range(DC):
                                    ins = te.matmul(
                                        PS[slot][:, :TC],
                                        wa[sj % 2][:, c * SUPA + fo * 128: c * SUPA + (fo + 1) * 128],
                                        u_src[:, c, cs:ce],
                                        start=(c == 0), stop=(c == DC - 1))
                                ins.then_inc(pe.h, 1); pe.v += 1
                            on('pe', mm_body)
                            mm_v = pe.v
                            last_mm = mm_v

                            def act_body(sc, f=f, cs=cs, ce=ce, slot=slot, mm_v=mm_v, fn=fn, ba=ba):
                                W(sc, 'ac', pe, mm_v)
                                W(sc, 'ac', hd, consts_hd)
                                gwait(sc, 'ac', ('hsb',))
                                sc.activation(out=h_sb[:, f * T + cs: f * T + ce],
                                              in_=PS[slot][:, :TC], func=fn,
                                              bias=ba[:, f:f + 1]).then_inc(ac.h, 1)
                                ac.v += 1
                            on('ac', act_body)
                            gset(('ps', slot), ac, ac.v)
                            act_v = ac.v

                            def mask_body(ve, f=f, cs=cs, ce=ce, act_v=act_v, mbf=mbf):
                                W(ve, 'dv', ac, act_v)
                                ve.tensor_mul(h_sb[:, f * T + cs: f * T + ce],
                                              h_sb[:, f * T + cs: f * T + ce],
                                              mbf[:, cs:ce]).then_inc(dv.h, 1)
                                dv.v += 1
                            on('dv', mask_body)
                    gset(('arena', sj % 2), pe, last_mm)
                    if sj + 2 < NSA:
                        emit_load_a(sj + 2)
                h_ready = dv.v
                SUPB = 128
                NSB = D // SUPB
                loadvb = {}

                def emit_load_b(es_, Wb_r=Wb_r, loadvb=loadvb):
                    def body(gp, es_=es_):
                        gwait(gp, 'pl', ('arena', es_ % 2))
                        w_ = pw[es_ % 2]
                        hc = FC // 2
                        gp.dma_start(out=wa[es_ % 2][:, :hc * SUPB],
                                     in_=Wb_r[:, :hc, es_ * SUPB:(es_ + 1) * SUPB]
                                     ).then_inc(w_.h, 16)
                        gp.dma_start(out=wa[es_ % 2][:, hc * SUPB:FC * SUPB],
                                     in_=Wb_r[:, hc:, es_ * SUPB:(es_ + 1) * SUPB]
                                     ).then_inc(w_.h, 16)
                        w_.v += 32
                        loadvb[es_] = (w_, w_.v)
                    on('pl', body)
                emit_load_b(0)
                if NSB > 1:
                    emit_load_b(1)
                for es_ in range(NSB):
                    e = es_
                    last_mm = 0
                    for tc in range(NTC):
                        cs, ce = tc * TC, (tc + 1) * TC
                        slot = 2 + (e * NTC + tc) % 2

                        def mmb(te, es_=es_, cs=cs, ce=ce, slot=slot, h_ready=h_ready):
                            W(te, 'pe', *loadvb[es_])
                            W(te, 'pe', dv, h_ready)
                            gwait(te, 'pe', ('ps', slot))
                            for j in range(FC):
                                ins = te.matmul(PS[slot][:, :TC],
                                                wa[es_ % 2][:, j * SUPB:(j + 1) * SUPB],
                                                h_sb[:, j * T + cs: j * T + ce],
                                                start=(j == 0), stop=(j == FC - 1))
                            ins.then_inc(pe.h, 1); pe.v += 1
                        on('pe', mmb)
                        mm_v = pe.v
                        last_mm = mm_v

                        def evac(ve, e=e, cs=cs, ce=ce, slot=slot, mm_v=mm_v, mf=mf, bb=bb):
                            W(ve, 'dv', pe, mm_v)
                            W(ve, 'dv', hd, consts_hd)
                            gwait(ve, 'dv', 'scr2')
                            ve.scalar_tensor_tensor(scr2[:, :TC], mf[:, cs:ce],
                                                    bb[:, e:e + 1], PS[slot][:, :TC],
                                                    ALU.mult, ALU.add).then_inc(dv.h, 1)
                            dv.v += 1
                            W(ve, 'dv', dv, dv.v)
                            if (e, cs) in xadd:
                                W(ve, 'dv', dv, xadd[(e, cs)])
                            ve.tensor_add(Xdst[:, e, cs:ce], Xdst[:, e, cs:ce],
                                          scr2[:, :TC]).then_inc(dv.h, 1)
                            dv.v += 1
                            xadd[(e, cs)] = dv.v
                            gset('scr2', dv, dv.v)
                        on('dv', evac)
                        gset(('ps', slot), dv, dv.v)
                    gset(('arena', es_ % 2), pe, last_mm)
                    if es_ + 2 < NSB:
                        emit_load_b(es_ + 2)
                gset(('hsb',), pe, last_mm)
            gset('uarena', pe, pe.v)
            ffn_done = dv.v
            close_scope(esF)
            return ffn_done

        esM1 = ExitStack()
        masks1 = tuple(esM1.enter_context(nc.sbuf_tensor(n, [128, T], d))
                       for n, d in (("m0bf", BF16), ("m1bf", BF16),
                                    ("m0f", F32), ("m1f", F32)))
        u1_v = emit_ln(X, masks1, (hd, x_ready), u_t)
        emit_ffn(X, u1_v, masks1, "f1", u_t)
        close_scope(esM1)

        u2_v = emit_ln(X, None, (dv, dv.v), u_t)

        def spill(sp):
            W(sp, 'sp', dv, u2_v)
            sp.dma_start(out=xsp_r, in_=X[:, :, :]).then_inc(hsp.h, 16)
            hsp.v += 16
        on('sp', spill)
        close_scope(esX)

        esQ = ExitStack()
        qs = esQ.enter_context(nc.sbuf_tensor("qs", [128, DC, T], BF16, side="right"))
        esB_ = ExitStack()
        sq = esB_.enter_context(nc.sbuf_tensor("sq", [128, DC, T], BF16, side="right"))
        raws = esB_.enter_context(nc.sbuf_tensor("raws", [1, M * T], BF16, side="right"))
        scr2b = esB_.enter_context(nc.sbuf_tensor("scr2b", [128, 512], BF16, side="right"))
        esA_ = ExitStack()
        kvst = esA_.enter_context(nc.sbuf_tensor("kvst", [128, DC * T], BF16, side="right"))
        inv_sqrt_hd = float(1.0 / np.sqrt(128.0))

        SUPQ = 256
        NSQ = D // SUPQ

        def emit_proj(Wh_r, kind):
            loadv = {}

            def emit_load(os_):
                def body(gp, os_=os_):
                    gwait(gp, 'pl', ('arena', os_ % 2))
                    w_ = pw[os_ % 2]
                    hc = DC // 2
                    gp.dma_start(out=wa[os_ % 2][:, :hc * SUPQ],
                                 in_=Wh_r[:, :hc, os_ * SUPQ:(os_ + 1) * SUPQ]
                                 ).then_inc(w_.h, 16)
                    gp.dma_start(out=wa[os_ % 2][:, hc * SUPQ:DC * SUPQ],
                                 in_=Wh_r[:, hc:, os_ * SUPQ:(os_ + 1) * SUPQ]
                                 ).then_inc(w_.h, 16)
                    w_.v += 32
                    loadv[os_] = (w_, w_.v)
                on('pl', body)
            emit_load(0)
            if NSQ > 1:
                emit_load(1)
            for os_ in range(NSQ):
                last_mm = 0
                if kind != 'v':
                    for ko in range(SUPQ // 128):
                        k = os_ * (SUPQ // 128) + ko
                        for tc in range(NTC):
                            cs, ce = tc * TC, (tc + 1) * TC
                            slot = (k * NTC + tc) % 2

                            def mmq(te, os_=os_, ko=ko, cs=cs, ce=ce, slot=slot):
                                W(te, 'pe', *loadv[os_])
                                W(te, 'pe', dv, u2_v)
                                gwait(te, 'pe', ('ps', slot))
                                for c in range(DC):
                                    ins = te.matmul(
                                        PS[slot][:, :TC],
                                        wa[os_ % 2][:, c * SUPQ + ko * 128: c * SUPQ + (ko + 1) * 128],
                                        u_t[:, c, cs:ce],
                                        start=(c == 0), stop=(c == DC - 1))
                                ins.then_inc(pe.h, 1); pe.v += 1
                            on('pe', mmq)
                            mm_v = pe.v
                            last_mm = mm_v
                            if kind == 'q':
                                def evq(sc, k=k, cs=cs, ce=ce, slot=slot, mm_v=mm_v):
                                    W(sc, 'ac', pe, mm_v)
                                    gwait(sc, 'ac', 'scr2b')
                                    gwait(sc, 'ac', 'scrfq')
                                    sc.activation(out=qs[:, k, cs:ce], in_=PS[slot][:, :TC],
                                                  func=AF.Copy, scale=inv_sqrt_hd)
                                    sc.activation(out=sq[:, k, cs:ce], in_=PS[slot][:, :TC],
                                                  func=AF.Relu)
                                    sc.activation(out=scrf[:, 512:512 + TC],
                                                  in_=PS[slot][:, :TC],
                                                  func=AF.Relu, scale=-1.0).then_inc(ac.h, 1)
                                    ac.v += 1
                                    W(sc, 'ac', ac, ac.v)
                                    sc.activation(out=scr2b[:, :TC],
                                                  in_=scrf[:, 512:512 + TC],
                                                  func=AF.Exp, scale=-1.0).then_inc(ac.h, 1)
                                    ac.v += 1
                                on('ac', evq)
                                gset(('ps', slot), ac, ac.v)
                                gset('scrfq', ac, ac.v)
                                e_v = ac.v

                                def addsig(ve, k=k, cs=cs, ce=ce, e_v=e_v):
                                    W(ve, 'dv', ac, e_v)
                                    ve.tensor_add(sq[:, k, cs:ce], sq[:, k, cs:ce],
                                                  scr2b[:, :TC]).then_inc(dv.h, 1)
                                    dv.v += 1
                                    gset('scr2b', dv, dv.v)
                                on('dv', addsig)
                            else:
                                def evk(sc, k=k, cs=cs, ce=ce, slot=slot, mm_v=mm_v):
                                    W(sc, 'ac', pe, mm_v)
                                    gwait(sc, 'ac', ('kvst',))
                                    sc.activation(out=kvst[:, k * T + cs: k * T + ce],
                                                  in_=PS[slot][:, :TC],
                                                  func=AF.Copy).then_inc(ac.h, 1)
                                    ac.v += 1
                                on('ac', evk)
                                gset(('ps', slot), ac, ac.v)
                else:
                    for tt in range(NOB):
                        slot = tt % 2

                        def mmv(te, os_=os_, tt=tt, slot=slot):
                            W(te, 'pe', *loadv[os_])
                            W(te, 'pe', dv, u2_v)
                            gwait(te, 'pe', ('ps', slot))
                            for c in range(DC):
                                ins = te.matmul(PS[slot][:, :SUPQ],
                                                u_t[:, c, tt * 128:(tt + 1) * 128],
                                                wa[os_ % 2][:, c * SUPQ:(c + 1) * SUPQ],
                                                start=(c == 0), stop=(c == DC - 1))
                            ins.then_inc(pe.h, 1); pe.v += 1
                        on('pe', mmv)
                        mm_v = pe.v
                        last_mm = mm_v

                        def evv(sc, os_=os_, tt=tt, slot=slot, mm_v=mm_v):
                            W(sc, 'ac', pe, mm_v)
                            gwait(sc, 'ac', ('kvst',))
                            sc.activation(out=kvst[:, tt * D + os_ * SUPQ: tt * D + (os_ + 1) * SUPQ],
                                          in_=PS[slot][:, :SUPQ],
                                          func=AF.Copy).then_inc(ac.h, 1)
                            ac.v += 1
                        on('ac', evv)
                        gset(('ps', slot), ac, ac.v)
                gset(('arena', os_ % 2), pe, last_mm)
                if os_ + 2 < NSQ:
                    emit_load(os_ + 2)

        emit_proj(wq_r, 'q')
        sig_done = dv.v
        q_done = ac.v
        emit_proj(wk_r, 'k')
        k_done = ac.v

        def send_k(sp):
            W(sp, 'sp', ac, k_done)
            sp.dma_start(out=kin_r,
                         in_=kvst[:].rearrange("p (c t) -> p c t", c=DC)
                         ).then_inc(hk.h, 16)
            hk.v += 16
        on('sp', send_k)
        gset(('kvst',), hk, hk.v)
        kin_v = hk.v

        emit_proj(wv_r, 'v')
        v_done = ac.v
        gset('uarena', pe, pe.v)

        def send_v(sp):
            W(sp, 'sp', ac, v_done)
            sp.dma_start(out=vin_r,
                         in_=kvst[:].rearrange("p (b d) -> p b d", b=NOB)
                         ).then_inc(hv.h, 16)
            hv.v += 16
        on('sp', send_v)
        vin_v = hv.v

        for m in range(M):
            for tc in range(NTC):
                cs, ce = tc * TC, (tc + 1) * TC
                slot = 4 + tc % 2

                def mmr(te, m=m, cs=cs, ce=ce, slot=slot):
                    W(te, 'pe', dv, sig_done)
                    W(te, 'pe', pd, consts_pd)
                    gwait(te, 'pe', ('ps', slot))
                    for c in range(DC):
                        ins = te.matmul(PS[slot][0:1, :TC], norms_t[:, c, m:m + 1],
                                        sq[:, c, cs:ce], start=(c == 0), stop=(c == DC - 1))
                    ins.then_inc(pe.h, 1); pe.v += 1
                on('pe', mmr)
                mm_v = pe.v

                def evr(ve, m=m, cs=cs, ce=ce, slot=slot, mm_v=mm_v, tc=tc):
                    W(ve, 'dv', pe, mm_v)
                    ve.tensor_copy(raws[0:1, m * T + cs: m * T + ce],
                                   PS[slot][0:1, :TC]).then_inc(dv.h, 1)
                    ve.reduce_sum(srow2[0:1, m * NTC + tc: m * NTC + tc + 1],
                                  PS[slot][0:1, :TC], axis=AX.X).then_inc(dv.h, 1)
                    dv.v += 2
                on('dv', evr)
                gset(('ps', slot), dv, dv.v)

        def relsum(ve):
            W(ve, 'dv', dv, dv.v)
            ve.memset(smin_t[0:1, 0:8], 0.0).then_inc(dv.h, 1)
            dv.v += 1
            W(ve, 'dv', dv, dv.v)
            if NTC == 2:
                ve.tensor_add(smin_t[0:1, 0:M],
                              srow2[0:1, 0:2 * M:2],
                              srow2[0:1, 1:2 * M:2]).then_inc(dv.h, 1)
            else:
                ve.tensor_copy(smin_t[0:1, 0:M], srow2[0:1, 0:M]).then_inc(dv.h, 1)
            dv.v += 1
        on('dv', relsum)
        rel_v = dv.v

        def send_sm(sp):
            W(sp, 'sp', dv, rel_v)
            sp.dma_start(out=smin.ap(), in_=smin_t[:]).then_inc(hsn.h, 16)
            hsn.v += 16
        on('sp', send_sm)
        sm_v = hsn.v

        def colls(gp):
            W(gp, 'pl', hsn, sm_v)
            gp.collective_compute("AllGather", ALU.bypass,
                                  replica_groups=[[0, 1], [2, 3], [4, 5], [6, 7]],
                                  ins=[smin.ap()], outs=[smg.ap()]).then_inc(cc2.h, 1)
            cc2.v += 1
            W(gp, 'pl', hk, kin_v)
            gp.collective_compute("AllGather", ALU.bypass,
                                  replica_groups=[[0, 1], [2, 3], [4, 5], [6, 7]],
                                  ins=[kin.ap()], outs=[kg.ap()]).then_inc(cc1.h, 1)
            cc1.v += 1
            W(gp, 'pl', hv, vin_v)
            gp.collective_compute("AllGather", ALU.bypass,
                                  replica_groups=[[0, 1], [2, 3], [4, 5], [6, 7]],
                                  ins=[vin.ap()], outs=[vg.ap()]).then_inc(cc1.h, 1)
            cc1.v += 1
        on('pl', colls)
        close_scope(esA_)
        close_scope(esU1)

        esP = ExitStack()
        pre = esP.enter_context(nc.sbuf_tensor("pre", [128, DC, T], F32))
        esS = ExitStack()
        kslab = [esS.enter_context(nc.sbuf_tensor(f"ks{o}", [128, 2, T], BF16, side="right"))
                 for o in range(2)]
        vslab = [esS.enter_context(nc.sbuf_tensor(f"vs{o}", [128, 2, NOB, 128], BF16, side="right"))
                 for o in range(2)]
        attnw = esS.enter_context(nc.sbuf_tensor("attnw", [128, 2, 4, 512], BF16, side="right"))
        lrow = esS.enter_context(nc.sbuf_tensor("lrow", [1, T], F32, side="right"))
        cbc = [esS.enter_context(nc.sbuf_tensor(f"cbc{i}", [128, T], BF16, side="right"))
               for i in range(2)]
        mscr = esS.enter_context(nc.sbuf_tensor("mscr", [128, 2, 512], F32, side="right"))
        ocol = esS.enter_context(nc.sbuf_tensor("ocol", [128, 1], F32, side="right"))
        wcol = esS.enter_context(nc.sbuf_tensor("wcol", [128, 4], F32, side="right"))

        def smrecv(sp):
            W(sp, 'sp', cc2, 1)
            sp.dma_start(out=smg_t0[:], in_=smg.ap()[0:1, :]).then_inc(hsm2.h, 16)
            sp.dma_start(out=smg_t1[:], in_=smg.ap()[1:2, :]).then_inc(hsm2.h, 16)
            hsm2.v += 32
        on('sp', smrecv)
        smg_v = hsm2.v

        def relcomb(ve):
            W(ve, 'dv', hsm2, smg_v)
            ve.tensor_add(srow[0:1, 0:M], smg_t0[0:1, 0:M],
                          smg_t1[0:1, 0:M]).then_inc(dv.h, 1)
            dv.v += 1
            W(ve, 'dv', dv, dv.v)
            ve.tensor_scalar_mul(srow[0:1, 0:M], srow[0:1, 0:M],
                                 1.0 / cfg.S).then_inc(dv.h, 1)
            dv.v += 1
            W(ve, 'dv', dv, dv.v)
            ve.reduce_max(srow[0:1, 8:9], srow[0:1, 0:M], axis=AX.X).then_inc(dv.h, 1)
            dv.v += 1
            W(ve, 'dv', dv, dv.v)
            ve.tensor_scalar_mul(srow[0:1, 9:10], srow[0:1, 8:9], -1.0).then_inc(dv.h, 1)
            dv.v += 1
        on('dv', relcomb)
        rr_v = dv.v

        def relexp(sc):
            W(sc, 'ac', dv, rr_v)
            W(sc, 'ac', hd, consts_hd)
            sc.activation(out=srow[0:1, 4:4 + M], in_=srow[0:1, 0:M],
                          func=AF.Exp, bias=srow[0:1, 9:10]).then_inc(ac.h, 1)
            sc.activation(out=gsig[:], in_=gate_t[:],
                          func=AF.Sigmoid).then_inc(ac.h, 1)
            ac.v += 2
        on('ac', relexp)
        re_v = ac.v

        def relw(ve):
            W(ve, 'dv', ac, re_v)
            ve.reduce_sum(srow[0:1, 10:11], srow[0:1, 4:4 + M], axis=AX.X).then_inc(dv.h, 1)
            dv.v += 1
            W(ve, 'dv', dv, dv.v)
            ve.reciprocal(srow[0:1, 11:12], srow[0:1, 10:11]).then_inc(dv.h, 1)
            dv.v += 1
            W(ve, 'dv', dv, dv.v)
            ve.tensor_scalar_mul(srow[0:1, 12:12 + M], srow[0:1, 4:4 + M],
                                 srow[0:1, 11:12]).then_inc(dv.h, 1)
            dv.v += 1
            W(ve, 'dv', dv, dv.v)
            ve.tensor_scalar_mul(srow[0:1, 12:12 + M], srow[0:1, 12:12 + M],
                                 gsig[0:1, 0:1]).then_inc(dv.h, 1)
            dv.v += 1
            W(ve, 'dv', dv, dv.v)
            ve.tensor_scalar(omg[:], gsig[:], -1.0, 1.0, ALU.mult, ALU.add).then_inc(dv.h, 1)
            dv.v += 1
        on('dv', relw)
        gw_v = dv.v

        def cbcol(te):
            W(te, 'pe', dv, gw_v)
            W(te, 'pe', pl, consts_pl)
            gwait(te, 'pe', ('ps', 6))
            te.matmul(PS[6][:, 0:1], ones1[:], omg[0:1, 0:1],
                      start=True, stop=True)
            te.matmul(PS[6][:, 1:5], ones1[:], srow[0:1, 12:12 + M],
                      start=True, stop=True).then_inc(pe.h, 1)
            pe.v += 1
        on('pe', cbcol)
        cbcol_v = pe.v

        def cbcol2(ve):
            W(ve, 'dv', pe, cbcol_v)
            ve.tensor_copy(ocol[:, :], PS[6][:, 0:1]).then_inc(dv.h, 1)
            ve.tensor_copy(wcol[:, :], PS[6][:, 1:5]).then_inc(dv.h, 1)
            dv.v += 2
        on('dv', cbcol2)
        gset(('ps', 6), dv, dv.v)

        SUPM = 256
        NSM = D // SUPM
        pre_w = {}
        mem_pairs = [(m, es_) for m in range(M) for es_ in range(NSM)]
        mem_loadv = {}

        def emit_load_m(p_i):
            m_, e_ = mem_pairs[p_i]

            def body(gp, p_i=p_i, m_=m_, e_=e_):
                gwait(gp, 'pl', ('arena', p_i % 2))
                w_ = pw[p_i % 2]
                hc = DC // 2
                gp.dma_start(out=wa[p_i % 2][:, :hc * SUPM],
                             in_=mems_r[m_, :, :hc, e_ * SUPM:(e_ + 1) * SUPM]
                             ).then_inc(w_.h, 16)
                gp.dma_start(out=wa[p_i % 2][:, hc * SUPM:DC * SUPM],
                             in_=mems_r[m_, :, hc:, e_ * SUPM:(e_ + 1) * SUPM]
                             ).then_inc(w_.h, 16)
                w_.v += 32
                mem_loadv[p_i] = (w_, w_.v)
            on('pl', body)

        mchain_i = [0]

        def memq_gen():
            emit_load_m(0)
            emit_load_m(1)
            for p_i, (m, es_) in enumerate(mem_pairs):
                if es_ == 0:
                    def crow(ve, m=m):
                        W(ve, 'dv', dv, gw_v)
                        es_lp = nc.allow_low_precision(reason="bf16 in-place clamp of raw row (reference-tolerant path)")
                        es_lp.__enter__()
                        ve.tensor_scalar_max(raws[0:1, m * T:(m + 1) * T],
                                             raws[0:1, m * T:(m + 1) * T], 1e-6).then_inc(dv.h, 1)
                        dv.v += 1
                        es_lp.__exit__(None, None, None)
                    yield ('crow', m, crow)
                    for tc2 in range(NTC):
                        def cbcast(te, m=m, tc2=tc2):
                            W(te, 'pe', dv, dv.v)
                            W(te, 'pe', pl, consts_pl)
                            gwait(te, 'pe', ('ps', 6))
                            te.matmul(PS[6][:, :TC], ones1b[:],
                                      raws[0:1, m * T + tc2 * TC: m * T + (tc2 + 1) * TC],
                                      start=True, stop=True).then_inc(pe.h, 1)
                            pe.v += 1
                        yield ('cbcast', m, tc2, cbcast)
                for eo in range(SUPM // 128):
                    e = es_ * (SUPM // 128) + eo
                    for tc2 in range(NTC):
                        cs, ce = tc2 * TC, (tc2 + 1) * TC
                        slot = 6 + mchain_i[0] % 2
                        mchain_i[0] += 1

                        def chain(te, p_i=p_i, eo=eo, cs=cs, ce=ce, slot=slot):
                            W(te, 'pe', *mem_loadv[p_i])
                            W(te, 'pe', dv, sig_done)
                            gwait(te, 'pe', ('ps', slot))
                            for c in range(DC):
                                ins = te.matmul(
                                    PS[slot][:, :TC],
                                    wa[p_i % 2][:, c * SUPM + eo * 128: c * SUPM + (eo + 1) * 128],
                                    sq[:, c, cs:ce],
                                    start=(c == 0), stop=(c == DC - 1))
                            ins.then_inc(pe.h, 1); pe.v += 1

                        def evm(ve, m=m, e=e, cs=cs, ce=ce, slot=slot):
                            half = slot - 6
                            ve.tensor_mul(mscr[:, half, :], PS[slot][:, :TC],
                                          cbc[m % 2][:, cs:ce]).then_inc(dv.h, 1)
                            dv.v += 1
                            W(ve, 'dv', dv, dv.v)
                            if (e, cs) in pre_w:
                                W(ve, 'dv', *pre_w[(e, cs)])
                            if m == 0:
                                ve.tensor_copy(pre[:, e, cs:ce], mscr[:, half, :]).then_inc(dv.h, 1)
                            else:
                                ve.tensor_add(pre[:, e, cs:ce], pre[:, e, cs:ce],
                                              mscr[:, half, :]).then_inc(dv.h, 1)
                            dv.v += 1
                            pre_w[(e, cs)] = (dv, dv.v)
                        yield ('chain', p_i, chain, evm, slot)
                if p_i + 2 < len(mem_pairs):
                    emit_load_m(p_i + 2)

        memq = memq_gen()
        mem_done = [False]

        def emit_one_memq():
            try:
                item = next(memq)
            except StopIteration:
                mem_done[0] = True
                return
            if item[0] == 'crow':
                on('dv', item[2])
            elif item[0] == 'cbcast':
                _, m, tc2, f = item
                on('pe', f)
                cb_v = pe.v

                def cbrec(ve, m=m, tc2=tc2, cb_v=cb_v):
                    W(ve, 'dv', pe, cb_v)
                    ve.reciprocal(scrf[:, 0:TC], PS[6][:, :TC]).then_inc(dv.h, 1)
                    dv.v += 1
                    W(ve, 'dv', dv, dv.v)
                    ve.tensor_scalar_mul(cbc[m % 2][:, tc2 * TC:(tc2 + 1) * TC],
                                         scrf[:, 0:TC],
                                         wcol[:, m:m + 1]).then_inc(dv.h, 1)
                    dv.v += 1
                on('dv', cbrec)
                gset(('ps', 6), dv, dv.v)
            else:
                _, p_i, chain, evm, slot = item
                on('pe', chain)
                mm_v = pe.v

                def evm_w(ve, evm=evm, mm_v=mm_v):
                    W(ve, 'dv', pe, mm_v)
                    evm(ve)
                on('dv', evm_w)
                gset(('ps', slot), dv, dv.v - 1)
                gset(('arena', p_i % 2), pe, mm_v)

        N_MEMQ = M * (3 + NSM * (SUPM // 128) * NTC)
        FRONT = 40
        for _ in range(FRONT):
            emit_one_memq()
        assert len(pre_w) >= DC * NTC, "front-load must cover memory 0 fully"

        def parts_of(tc):
            out = []
            cs0, ce0 = tc * TC, (tc + 1) * TC
            for j in range(NB):
                a = max(cs0, ustart_blk[j] * 128)
                if a < ce0:
                    out.append((j, a, ce0))
            return out

        packed = {}
        for tc in range(NTC):
            parts = parts_of(tc)
            slices = []
            cur, cur_w = [], 0
            for (j, a, b) in parts:
                w = b - a
                if cur_w + w > 512:
                    slices.append(cur); cur, cur_w = [], 0
                cur.append((j, a, b, cur_w)); cur_w += w
            if cur:
                slices.append(cur)
            groups = [slices[i:i + 4] for i in range(0, len(slices), 4)]
            packed[tc] = (parts, groups)

        aw_guard = {}
        memq_budget = [0.0]
        n_groups_total = DC * sum(len(packed[tc][1]) for tc in range(NTC))
        memq_per_group = float(N_MEMQ - FRONT) / max(1, n_groups_total)

        slab_v = 0
        slab_hist = {}
        for h in range(DC):
            par = h % 2

            def slabload(sp, h=h, par=par):
                W(sp, 'sp', cc1, 2)
                gwait(sp, 'sp', ('slab', par))
                for o in range(2):
                    sp.dma_start(out=kslab[o][:, par, :],
                                 in_=kg_r[:, o * DC + h, :]).then_inc(hsl.h, 16)
                    sp.dma_start(out=vslab[o][:, par, :, :],
                                 in_=vg_r[:, o * NOB:(o + 1) * NOB,
                                          h * 128:(h + 1) * 128]).then_inc(hsl.h, 16)
                    hsl.v += 32
            on('sp', slabload)
            slab_v = hsl.v
            slab_last = 0
            for tc in range(NTC):
                parts, groups = packed[tc]
                first_j = parts[0][0]
                last_j = parts[-1][0]
                for gi, grp in enumerate(groups):
                    n_sl = len(grp)
                    def scores(te, h=h, par=par, grp=grp, slab_v=slab_v):
                        W(te, 'pe', hsl, slab_v)
                        W(te, 'pe', ac, q_done)
                        gwait(te, 'pe', ('sc',))
                        for bi in range(4):
                            gwait(te, 'pe', ('ps', bi))
                        for si, sl in enumerate(grp):
                            for (j, a, b, off) in sl:
                                o = owner_of[j]
                                pj = pos_of[j]
                                ins = te.matmul(SCP[:, si, off:off + b - a],
                                                kslab[o][:, par, pj * 128:(pj + 1) * 128],
                                                qs[:, h, a:b],
                                                start=True, stop=True)
                        ins.then_inc(pe.h, 1); pe.v += 1
                    on('pe', scores)
                    sc_v = pe.v
                    slab_last = sc_v
                    memq_budget[0] += memq_per_group
                    while memq_budget[0] >= 1.0 and not mem_done[0]:
                        emit_one_memq()
                        memq_budget[0] -= 1.0

                    half = (h * 16 + tc * 8 + gi) % 2

                    def expg(sc_, sc_v=sc_v, half=half, n_sl=n_sl):
                        W(sc_, 'ac', pe, sc_v)
                        W(sc_, 'ac', pl, consts_pl)
                        gwait(sc_, 'ac', ('aw', half))
                        sc_.activation(
                            out=attnw[:, half, 0:n_sl, :].rearrange("p a b -> p (a b)"),
                            in_=SCP[:, 0:n_sl, :].rearrange("p a b -> p (a b)"),
                            func=AF.Exp, bias=shift_t[:, 0:1]).then_inc(ac.h, 1)
                        ac.v += 1
                    on('ac', expg)
                    exp_v = ac.v
                    gset(('sc',), ac, exp_v)

                    mask_ops = []
                    for si, sl in enumerate(grp):
                        for (j, a, b, off) in sl:
                            ms = ustart_blk[j] * 128
                            me = min(ms + 256, T)
                            x0, x1 = max(a, ms), min(b, me)
                            if x0 < x1:
                                mask_ops.append((si, j, a, off, x0, x1, ms))
                    mask_v = 0
                    if mask_ops:
                        def maskg(ve, mask_ops=mask_ops, half=half, exp_v=exp_v):
                            W(ve, 'dv', ac, exp_v)
                            W(ve, 'dv', hd, consts_hd)
                            for (si, j, a, off, x0, x1, ms) in mask_ops:
                                ins = ve.tensor_mul(
                                    attnw[:, half, si, off + x0 - a: off + x1 - a],
                                    attnw[:, half, si, off + x0 - a: off + x1 - a],
                                    amask_t[:, j, x0 - ms: x1 - ms])
                            ins.then_inc(dv.h, 1); dv.v += 1
                        on('dv', maskg)
                        mask_v = dv.v

                    def lav(te, h=h, tc=tc, par=par, grp=grp, half=half, exp_v=exp_v,
                            mask_v=mask_v, first_j=first_j, last_j=last_j):
                        W(te, 'pe', ac, exp_v)
                        if mask_v:
                            W(te, 'pe', dv, mask_v)
                        gwait(te, 'pe', ('ps', 4))
                        gwait(te, 'pe', ('ps', 5))
                        co = tc * TC
                        for si, sl in enumerate(grp):
                            for (j, a, b, off) in sl:
                                o = owner_of[j]
                                pj = pos_of[j]
                                st = (j == first_j)
                                sp_ = (j == last_j)
                                te.matmul(PS[5][0:1, a - co: b - co],
                                          onescol[:],
                                          attnw[:, half, si, off:off + b - a],
                                          start=st, stop=sp_)
                                ins = te.matmul(PS[4][:, a - co: b - co],
                                                vslab[o][:, par, pj, :],
                                                attnw[:, half, si, off:off + b - a],
                                                start=st, stop=sp_)
                        ins.then_inc(pe.h, 1); pe.v += 1
                    on('pe', lav)
                    aw_guard[half] = pe.v
                    gset(('aw', half), pe, pe.v)
                    slab_last = pe.v
                lav_v = pe.v

                def lep(ve, tc=tc, lav_v=lav_v):
                    W(ve, 'dv', pe, lav_v)
                    ve.tensor_copy(lrow[0:1, tc * TC:(tc + 1) * TC],
                                   PS[5][0:1, :TC]).then_inc(dv.h, 1)
                    dv.v += 1
                on('dv', lep)
                lep_v = dv.v

                def lbc(te, tc=tc, lep_v=lep_v):
                    W(te, 'pe', dv, lep_v)
                    te.matmul(PS[5][:, :TC], ones1[:],
                              lrow[0:1, tc * TC:(tc + 1) * TC],
                              start=True, stop=True).then_inc(pe.h, 1)
                    pe.v += 1
                on('pe', lbc)
                lbc_v = pe.v

                def avev(ve, h=h, tc=tc, lbc_v=lbc_v):
                    cs, ce = tc * TC, (tc + 1) * TC
                    W(ve, 'dv', pe, lbc_v)
                    gwait(ve, 'dv', 'scrfq')
                    ve.reciprocal(scrf[:, 0:TC], PS[5][:, :TC]).then_inc(dv.h, 1)
                    dv.v += 1
                    W(ve, 'dv', dv, dv.v)
                    ve.tensor_mul(scrf[:, 512:512 + TC], PS[4][:, :TC],
                                  scrf[:, 0:TC]).then_inc(dv.h, 1)
                    dv.v += 1
                    W(ve, 'dv', dv, dv.v)
                    if (h, cs) in pre_w:
                        W(ve, 'dv', *pre_w[(h, cs)])
                    ve.scalar_tensor_tensor(pre[:, h, cs:ce], scrf[:, 512:512 + TC],
                                            ocol[:, 0:1], pre[:, h, cs:ce],
                                            ALU.mult, ALU.add).then_inc(dv.h, 1)
                    dv.v += 1
                    pre_w[(h, cs)] = (dv, dv.v)
                    gset('scrfq', dv, dv.v)
                on('dv', avev)
                gset(('ps', 4), dv, dv.v)
                gset(('ps', 5), dv, dv.v)
            slab_hist[h] = slab_last
            if h >= 1:
                gset(('slab', (h + 1) % 2), pe, slab_hist[h - 1])

        while not mem_done[0]:
            emit_one_memq()
        for bi in range(4):
            gset(('ps', bi), ac, ac.v)
        close_scope(esS)
        close_scope(esB_)
        close_scope(esQ)

        esX2 = ExitStack()
        X2 = esX2.enter_context(nc.sbuf_tensor("X2", [128, DC, T], F32, side="right"))
        esU2 = ExitStack()
        pb = esU2.enter_context(nc.sbuf_tensor("pb", [128, DC, T], BF16, side="right"))
        xrel = {}

        def xreload(sp):
            W(sp, 'sp', hsp, 16)
            half = DC // 2
            q4 = DC // 4
            for qi in range(2):
                sp.dma_start(out=X2[:, qi * q4:(qi + 1) * q4, :],
                             in_=xsp_r[:, qi * q4:(qi + 1) * q4, :]).then_inc(hx2[0].h, 16)
                hx2[0].v += 16
            for e in range(half):
                xrel[e] = (hx2[0], hx2[0].v)
            for qi in range(2, 4):
                sp.dma_start(out=X2[:, qi * q4:(qi + 1) * q4, :],
                             in_=xsp_r[:, qi * q4:(qi + 1) * q4, :]).then_inc(hx2[1].h, 16)
                hx2[1].v += 16
            for e in range(half, DC):
                xrel[e] = (hx2[1], hx2[1].v)
        on('sp', xreload)

        def prebf(ve):
            for c in range(DC):
                for tc in range(NTC):
                    cs, ce = tc * TC, (tc + 1) * TC
                    W(ve, 'dv', *pre_w[(c, cs)])
                    ve.tensor_copy(pb[:, c, cs:ce], pre[:, c, cs:ce]).then_inc(dv.h, 1)
                    dv.v += 1
        on('dv', prebf)
        prebf_v = dv.v

        SUPO = 256
        NSO = D // SUPO
        loadvo = {}

        def emit_load_o(os_):
            def body(gp, os_=os_):
                gwait(gp, 'pl', ('arena', os_ % 2))
                w_ = pw[os_ % 2]
                hc = DC // 2
                gp.dma_start(out=wa[os_ % 2][:, :hc * SUPO],
                             in_=wo_r[:, :hc, os_ * SUPO:(os_ + 1) * SUPO]
                             ).then_inc(w_.h, 16)
                gp.dma_start(out=wa[os_ % 2][:, hc * SUPO:DC * SUPO],
                             in_=wo_r[:, hc:, os_ * SUPO:(os_ + 1) * SUPO]
                             ).then_inc(w_.h, 16)
                w_.v += 32
                loadvo[os_] = (w_, w_.v)
            on('pl', body)
        emit_load_o(0)
        if NSO > 1:
            emit_load_o(1)
        for os_ in range(NSO):
            last_mm = 0
            for ko in range(SUPO // 128):
                k = os_ * (SUPO // 128) + ko
                for tc in range(NTC):
                    cs, ce = tc * TC, (tc + 1) * TC
                    slot = (k * NTC + tc) % 2

                    def mmo(te, os_=os_, ko=ko, cs=cs, ce=ce, slot=slot):
                        W(te, 'pe', *loadvo[os_])
                        W(te, 'pe', dv, prebf_v)
                        gwait(te, 'pe', ('ps', slot))
                        for c in range(DC):
                            ins = te.matmul(
                                PS[slot][:, :TC],
                                wa[os_ % 2][:, c * SUPO + ko * 128: c * SUPO + (ko + 1) * 128],
                                pb[:, c, cs:ce],
                                start=(c == 0), stop=(c == DC - 1))
                        ins.then_inc(pe.h, 1); pe.v += 1
                    on('pe', mmo)
                    mm_v = pe.v
                    last_mm = mm_v

                    def evo(ve, k=k, cs=cs, ce=ce, slot=slot, mm_v=mm_v):
                        W(ve, 'dv', pe, mm_v)
                        W(ve, 'dv', *xrel[k])
                        ve.tensor_add(X2[:, k, cs:ce], X2[:, k, cs:ce],
                                      PS[slot][:, :TC]).then_inc(dv.h, 1)
                        dv.v += 1
                    on('dv', evo)
                    gset(('ps', slot), dv, dv.v)
            gset(('arena', os_ % 2), pe, last_mm)
            if os_ + 2 < NSO:
                emit_load_o(os_ + 2)
        close_scope(esU2)
        close_scope(esP)
        guards.pop('uarena', None)
        gset('uarena', pe, pe.v)

        esU3 = ExitStack()
        u3_t = esU3.enter_context(nc.sbuf_tensor("ub", [128, DC, T], BF16))
        esM2 = ExitStack()
        masks2 = tuple(esM2.enter_context(nc.sbuf_tensor(n, [128, T], d))
                       for n, d in (("n0bf", BF16), ("n1bf", BF16),
                                    ("n0f", F32), ("n1f", F32)))
        u3_v = emit_ln(X2, masks2, (dv, dv.v), u3_t)
        ffn2_v = emit_ffn(X2, u3_v, masks2, "f2", u3_t)
        close_scope(esM2)
        close_scope(esU3)

        def store(sp):
            W(sp, 'sp', dv, ffn2_v)
            sp.dma_start(out=out_r, in_=X2[:, :, :]).then_inc(hout.h, 16)
            hout.v += 16
            sp.wait_ge(hout.h, hout.v)
        on('sp', store)
        esX2.close()
    return nc


def _host_amask(sub, cfg):
    NB, NOB = cfg.NB, cfg.NOB
    OB0, OB1 = own_blocks(0, NB), own_blocks(1, NB)
    myOB = OB0 if sub == 0 else OB1
    am = np.ones((128, NB, 256), np.float32)
    for j in range(NB):
        us = min(bisect_left(OB0, j), bisect_left(OB1, j))
        mys = bisect_left(myOB, j)
        delta = mys - us
        if us >= NOB:
            continue
        am[:, j, :delta * 128] = 0.0
        if mys < NOB and myOB[mys] == j:
            tri = (np.arange(128)[:, None] <= np.arange(128)[None, :])
            am[:, j, delta * 128:(delta + 1) * 128] = tri.astype(np.float32)
    import ml_dtypes
    return am.astype(ml_dtypes.bfloat16)


def kernel(L0_attn_x, L0_attn_w_q, L0_attn_w_k, L0_attn_w_v, L0_attn_w_o,
           L0_attn_gate, L0_attn_memories, L0_attn_memory_norms,
           L0_ffn_Wg, L0_ffn_W0a, L0_ffn_b0a, L0_ffn_W0b, L0_ffn_b0b,
           L0_ffn_W1a, L0_ffn_b1a, L0_ffn_W1b, L0_ffn_b1b):
    cfg = CFG()
    D, F, NB, DC, FC, T = cfg.D, cfg.F, cfg.NB, cfg.DC, cfg.FC, cfg.T
    x = np.asarray(L0_attn_x, np.float32)
    ct = lambda a: np.ascontiguousarray(np.asarray(a, np.float32))
    shared = {
        "wqT": ct(np.asarray(L0_attn_w_q).T),
        "wkT": ct(np.asarray(L0_attn_w_k).T),
        "wvT": ct(np.asarray(L0_attn_w_v).T),
        "woT": ct(np.asarray(L0_attn_w_o).T),
        "W0aT": ct(np.asarray(L0_ffn_W0a).T),
        "W0bT": ct(np.asarray(L0_ffn_W0b).T),
        "W1aT": ct(np.asarray(L0_ffn_W1a).T),
        "W1bT": ct(np.asarray(L0_ffn_W1b).T),
        "WgT": ct(np.asarray(L0_ffn_Wg).T),
        "mems": ct(L0_attn_memories),
        "normsT": ct(np.asarray(L0_attn_memory_norms).T),
        "b0a2": ct(np.asarray(L0_ffn_b0a).reshape(FC, 128).T),
        "b1a2": ct(np.asarray(L0_ffn_b1a).reshape(FC, 128).T),
        "b0b2": ct(np.asarray(L0_ffn_b0b).reshape(DC, 128).T),
        "b1b2": ct(np.asarray(L0_ffn_b1b).reshape(DC, 128).T),
        "gate": ct(np.asarray(L0_attn_gate).reshape(1, 1)),
    }
    amasks = [_host_amask(0, cfg), _host_amask(1, cfg)]
    in_maps = []
    rows_per_core = []
    for c in range(8):
        b, sub = c // 2, c % 2
        ob = own_blocks(sub, NB)
        rows = np.concatenate([np.arange(g * 128, (g + 1) * 128) for g in ob])
        rows_per_core.append((b, rows))
        m = dict(shared)
        m["x"] = np.ascontiguousarray(x[b][rows].T)
        m["amask"] = amasks[sub]
        in_maps.append(m)

    nc = build(cfg)
    res = run_bass_kernel_spmd(nc, in_maps, core_ids=list(range(8)))
    out = np.empty_like(x)
    for c in range(8):
        b, rows = rows_per_core[c]
        out[b][rows] = res.results[c]["out"].T
    return out


# version 08
# speedup vs baseline: 1.1133x; 1.1133x over previous
import numpy as np
from bisect import bisect_left
from contextlib import ExitStack
import concourse.bass as bass
import concourse.mybir as mybir
from concourse.bass_utils import run_bass_kernel_spmd

F32 = mybir.dt.float32
BF16 = mybir.dt.bfloat16
AF = mybir.ActivationFunctionType
ALU = mybir.AluOpType
AX = mybir.AxisListType

EXP_SHIFT = -20.0


class CFG:
    def __init__(self, D=2048, F=4096, S=2048, B=4, M=4):
        self.D, self.F, self.S, self.B, self.M = D, F, S, B, M
        self.DC = D // 128
        self.FC = F // 128
        self.NB = S // 128
        self.NOB = self.NB // 2
        self.T = self.NOB * 128
        self.TC = min(512, self.T)
        self.NTC = self.T // self.TC
        self.LC = min(256, self.T)
        self.NLC = self.T // self.LC


def own_blocks(sub, NB):
    return [g for g in range(NB) if ((g % 4) in (0, 3)) == (sub == 0)]


class Ctr:
    def __init__(self, es, nc, name):
        self.h = es.enter_context(nc.semaphore(name))
        self.v = 0
        self.name = name


def build(cfg):
    nc = bass.Bass("TRN2", target_bir_lowering=False, debug=False)
    D, F, T, DC, FC, M = cfg.D, cfg.F, cfg.T, cfg.DC, cfg.FC, cfg.M
    NB, NOB, TC, NTC = cfg.NB, cfg.NOB, cfg.TC, cfg.NTC
    LC, NLC = cfg.LC, cfg.NLC
    OB0, OB1 = own_blocks(0, NB), own_blocks(1, NB)
    ustart_blk = [min(bisect_left(OB0, j), bisect_left(OB1, j)) for j in range(NB)]
    owner_of = {}
    pos_of = {}
    for s_, ob in ((0, OB0), (1, OB1)):
        for p_, g in enumerate(ob):
            owner_of[g] = s_
            pos_of[g] = p_

    dr = lambda n, s, d, k="ExternalInput": nc.dram_tensor(n, s, d, kind=k)
    x_h = dr("x", [D, T], F32)
    out_h = dr("out", [D, T], F32, "ExternalOutput")
    wqT = dr("wqT", [D, D], F32)
    wkT = dr("wkT", [D, D], F32)
    wvT = dr("wvT", [D, D], F32)
    woT = dr("woT", [D, D], F32)
    W0aT_h = dr("W0aT", [D, F], F32)
    W0bT_h = dr("W0bT", [F, D], F32)
    W1aT_h = dr("W1aT", [D, F], F32)
    W1bT_h = dr("W1bT", [F, D], F32)
    WgT_h = dr("WgT", [D, 2], F32)
    mems_h = dr("mems", [M, D, D], F32)
    normsT_h = dr("normsT", [D, M], F32)
    b0a_h = dr("b0a2", [128, FC], F32)
    b1a_h = dr("b1a2", [128, FC], F32)
    b0b_h = dr("b0b2", [128, DC], F32)
    b1b_h = dr("b1b2", [128, DC], F32)
    gate_h = dr("gate", [1, 1], F32)
    amask_h = dr("amask", [128, NB, 256], BF16)
    xsp = dr("xsp", [D, T], F32, "Internal")
    kin = dr("kin", [D, T], BF16, "Internal")
    vin = dr("vin", [T, D], BF16, "Internal")
    kg = dr("kg", [2 * D, T], BF16, "Internal")
    vg = dr("vg", [2 * T, D], BF16, "Internal")
    smin = dr("smin", [1, 8], F32, "Internal")
    smg = dr("smg", [2, 8], F32, "Internal")

    r3 = lambda h: h.ap().rearrange("(c p) n -> p c n", p=128)
    x_r = r3(x_h); out_r = r3(out_h); xsp_r = r3(xsp)
    kin_r = r3(kin); vin_r = r3(vin)
    wq_r = r3(wqT); wk_r = r3(wkT); wv_r = r3(wvT); wo_r = r3(woT)
    W0a_r = r3(W0aT_h); W1a_r = r3(W1aT_h)
    W0b_r = r3(W0bT_h); W1b_r = r3(W1bT_h)
    wg_r = r3(WgT_h); norms_r = r3(normsT_h)
    mems_r = mems_h.ap().rearrange("m (c p) e -> m p c e", p=128)
    kg_r = r3(kg)
    vg_r = r3(vg)
    amask_r = amask_h.ap()

    waited = {}
    guards = {}
    floor = []

    def W(eng, engname, ctr, val):
        if val <= 0:
            return
        k = (engname, ctr.name)
        if waited.get(k, 0) >= val:
            return
        eng.wait_ge(ctr.h, val)
        waited[k] = val

    def gwait(eng, engname, key):
        g = guards.get(key)
        if g is not None:
            W(eng, engname, g[0], g[1])

    def gset(key, ctr, val):
        guards[key] = (ctr, val)

    ACOLS = max(DC * 256, FC * 128, 16)

    es0 = ExitStack()
    with es0:
        sb = lambda n, s, d: es0.enter_context(nc.sbuf_tensor(n, s, d))
        ones1 = sb("ones1", [1, 128], F32)
        ones1b = sb("ones1b", [1, 128], BF16)
        onescol = sb("onescol", [128, 1], BF16)
        b0a_t = sb("b0a_t", [128, FC], F32)
        b1a_t = sb("b1a_t", [128, FC], F32)
        b0b_t = sb("b0b_t", [128, DC], F32)
        b1b_t = sb("b1b_t", [128, DC], F32)
        wg_t = sb("wg_t", [128, DC, 2], BF16)
        norms_t = sb("norms_t", [128, DC, M], BF16)
        gate_t = sb("gate_t", [1, 1], F32)
        gsig = sb("gsig", [1, 1], F32)
        omg = sb("omg", [1, 1], F32)
        srow = sb("srow", [1, 16], F32)
        srow2 = sb("srow2", [1, 16], F32)
        wa = [sb("wa0", [128, ACOLS], BF16), sb("wa1", [128, ACOLS], BF16)]
        scrf = sb("scrf", [128, 1024], F32)
        smin_t = sb("smin_t", [1, 8], F32)
        eps_t = sb("eps_t", [1, 1], F32)
        shift_t = sb("shift_t", [128, 1], F32)
        smg_t0 = sb("smg_t0", [1, 8], F32)
        smg_t1 = sb("smg_t1", [1, 8], F32)
        amask_t = sb("amask_t", [128, NB, 256], BF16)

        SCP = es0.enter_context(nc.psum_tensor("scp", [128, 4, 512], F32))
        PS = [SCP[:, i, :] for i in range(4)] + [
            es0.enter_context(nc.psum_tensor(f"ps{i}", [128, 512], F32))
            for i in range(4, 8)]

        hd = Ctr(es0, nc, "hd")
        pd = Ctr(es0, nc, "pd")
        pw = [Ctr(es0, nc, "pw0"), Ctr(es0, nc, "pw1")]
        hsp = Ctr(es0, nc, "hsp")
        hk = Ctr(es0, nc, "hk")
        hv = Ctr(es0, nc, "hv")
        hsn = Ctr(es0, nc, "hsn")
        hsm2 = Ctr(es0, nc, "hsm2")
        hsl = Ctr(es0, nc, "hsl")
        hsl2 = Ctr(es0, nc, "hsl2")
        hx2 = [Ctr(es0, nc, "hx2a"), Ctr(es0, nc, "hx2b")]
        hout = Ctr(es0, nc, "hout")
        pe = Ctr(es0, nc, "pe")
        ac = Ctr(es0, nc, "ac")
        dv = Ctr(es0, nc, "dv")
        pl = Ctr(es0, nc, "pl")
        cc1 = Ctr(es0, nc, "cc1")
        cc2 = Ctr(es0, nc, "cc2")

        all_ctrs = [hd, pd, pe, ac, dv, pl, hsp, hk, hv, hsn, hsm2, hsl, hsl2, hout] + pw + hx2
        block = es0.enter_context(nc.Block())
        disp = {'pe': block.tensor, 'ac': block.scalar, 'dv': block.vector,
                'pl': block.gpsimd, 'sp': block.sync}

        def on(engname, f):
            def g(eng):
                for (c, v) in floor:
                    W(eng, engname, c, v)
                f(eng)
            disp[engname](g)

        def close_scope(es):
            es.close()
            for c in all_ctrs:
                floor.append((c, c.v))

        esU1 = ExitStack()
        u_t = esU1.enter_context(nc.sbuf_tensor("ua", [128, DC, T], BF16))
        esX = ExitStack()
        X = esX.enter_context(nc.sbuf_tensor("X", [128, DC, T], F32))

        def p0(sp):
            q4 = DC // 4
            for qi in range(4):
                sp.dma_start(out=X[:, qi * q4:(qi + 1) * q4, :],
                             in_=x_r[:, qi * q4:(qi + 1) * q4, :]).then_inc(hd.h, 16)
                hd.v += 16
            for t, src in ((b0a_t[:], b0a_h.ap()), (b1a_t[:], b1a_h.ap()),
                           (b0b_t[:], b0b_h.ap()), (b1b_t[:], b1b_h.ap()),
                           (gate_t[:], gate_h.ap()), (amask_t[:], amask_r)):
                sp.dma_start(out=t, in_=src).then_inc(hd.h, 16)
                hd.v += 16
        on('sp', p0)
        consts_hd = hd.v
        x_ready = consts_hd

        def p0b(gp):
            gp.dma_start(out=wg_t[:], in_=wg_r).then_inc(pd.h, 16)
            gp.dma_start(out=norms_t[:], in_=norms_r).then_inc(pd.h, 16)
            pd.v += 32
            gp.memset(ones1[:], 1.0).then_inc(pl.h, 1)
            gp.memset(ones1b[:], 1.0).then_inc(pl.h, 1)
            pl.v += 1
            gp.memset(onescol[:], 1.0).then_inc(pl.h, 1)
            gp.memset(eps_t[:], 1e-5).then_inc(pl.h, 1)
            gp.memset(shift_t[:], EXP_SHIFT).then_inc(pl.h, 1)
            pl.v += 4
        on('pl', p0b)
        consts_pd = pd.v
        consts_pl = pl.v

        ln_i = [0]

        def emit_ln(Xsrc, masks, x_req, u_dst):
            ln_i[0] += 1
            esL = ExitStack()
            scr_b = esL.enter_context(nc.sbuf_tensor(f"lnb{ln_i[0]}", [128, 2 * DC * LC], BF16))
            rows = esL.enter_context(nc.sbuf_tensor(f"lnr{ln_i[0]}", [1, 8 * LC], F32))
            inv_d = 1.0 / D
            xqoff = DC * LC
            for t in range(NLC):
                cs, ce = t * LC, (t + 1) * LC

                def cast_body(sc, cs=cs, ce=ce):
                    W(sc, 'ac', *x_req)
                    gwait(sc, 'ac', 'lnscr')
                    for c in range(DC):
                        sc.activation(out=scr_b[:, c * LC:(c + 1) * LC],
                                      in_=Xsrc[:, c, cs:ce], func=AF.Copy)
                        ins = sc.activation(out=scr_b[:, xqoff + c * LC: xqoff + (c + 1) * LC],
                                            in_=Xsrc[:, c, cs:ce], func=AF.Square)
                    ins.then_inc(ac.h, 1); ac.v += 1
                on('ac', cast_body)
                cast_v = ac.v

                def stat_mm(te):
                    W(te, 'pe', ac, cast_v)
                    W(te, 'pe', pl, consts_pl)
                    gwait(te, 'pe', ('ps', 4))
                    gwait(te, 'pe', ('ps', 5))
                    for c in range(DC):
                        te.matmul(PS[4][0:1, :LC], onescol[:],
                                  scr_b[:, c * LC:(c + 1) * LC],
                                  start=(c == 0), stop=(c == DC - 1))
                    for c in range(DC):
                        ins = te.matmul(PS[5][0:1, :LC], onescol[:],
                                        scr_b[:, xqoff + c * LC: xqoff + (c + 1) * LC],
                                        start=(c == 0), stop=(c == DC - 1))
                    ins.then_inc(pe.h, 1); pe.v += 1
                on('pe', stat_mm)
                mm_v = pe.v
                gset('lnscr', pe, mm_v)

                mean_r = rows[0:1, 0:LC]
                var_r = rows[0:1, LC:2 * LC]
                msq_r = rows[0:1, 2 * LC:3 * LC]
                rstd_r = rows[0:1, 3 * LC:4 * LC]

                def stats(ve):
                    W(ve, 'dv', pe, mm_v)
                    gwait(ve, 'dv', 'lnrows')
                    ve.tensor_scalar_mul(mean_r, PS[4][0:1, :LC], inv_d).then_inc(dv.h, 1)
                    dv.v += 1
                    W(ve, 'dv', dv, dv.v)
                    ve.tensor_mul(msq_r, mean_r, mean_r).then_inc(dv.h, 1)
                    dv.v += 1
                    W(ve, 'dv', dv, dv.v)
                    ve.scalar_tensor_tensor(var_r, PS[5][0:1, :LC], inv_d,
                                            msq_r, ALU.mult, ALU.subtract).then_inc(dv.h, 1)
                    dv.v += 1
                on('dv', stats)
                gset(('ps', 4), dv, dv.v)
                gset(('ps', 5), dv, dv.v)
                var_v = dv.v

                def rstd1(sc):
                    W(sc, 'ac', dv, var_v)
                    W(sc, 'ac', pl, consts_pl)
                    sc.activation(out=msq_r, in_=var_r, func=AF.Sqrt,
                                  bias=eps_t[0:1, 0:1]).then_inc(ac.h, 1)
                    ac.v += 1
                on('ac', rstd1)
                sq_v = ac.v

                def rstd2(ve):
                    W(ve, 'dv', ac, sq_v)
                    ve.reciprocal(rstd_r, msq_r).then_inc(dv.h, 1)
                    dv.v += 1
                on('dv', rstd2)
                rstd_v = dv.v

                def bcast(te):
                    W(te, 'pe', dv, rstd_v)
                    gwait(te, 'pe', ('ps', 6))
                    gwait(te, 'pe', ('ps', 7))
                    te.matmul(PS[6][:, :LC], ones1[:], mean_r,
                              start=True, stop=True).then_inc(pe.h, 1)
                    te.matmul(PS[7][:, :LC], ones1[:], rstd_r,
                              start=True, stop=True).then_inc(pe.h, 1)
                    pe.v += 2
                on('pe', bcast)
                bc_v = pe.v

                def ubuild(ve, cs=cs, ce=ce):
                    W(ve, 'dv', pe, bc_v)
                    gwait(ve, 'dv', 'uarena')
                    for c in range(DC):
                        ve.tensor_sub(scrf[:, 0:LC], Xsrc[:, c, cs:ce],
                                      PS[6][:, :LC]).then_inc(dv.h, 1)
                        dv.v += 1
                        W(ve, 'dv', dv, dv.v)
                        ve.tensor_mul(u_dst[:, c, cs:ce], scrf[:, 0:LC],
                                      PS[7][:, :LC]).then_inc(dv.h, 1)
                        dv.v += 1
                        W(ve, 'dv', dv, dv.v)
                on('dv', ubuild)
                gset(('ps', 6), dv, dv.v)
                gset(('ps', 7), dv, dv.v)
                gset('lnrows', dv, dv.v)
                u_v = dv.v

                if masks is not None:
                    m0bf, m1bf, m0f, m1f = masks

                    def logit_mm(te, cs=cs, ce=ce):
                        W(te, 'pe', dv, u_v)
                        W(te, 'pe', pd, consts_pd)
                        gwait(te, 'pe', ('ps', 4))
                        gwait(te, 'pe', ('ps', 5))
                        for c in range(DC):
                            te.matmul(PS[4][0:1, :LC], wg_t[:, c, 0:1],
                                      u_dst[:, c, cs:ce], start=(c == 0), stop=(c == DC - 1))
                        for c in range(DC):
                            ins = te.matmul(PS[5][0:1, :LC], wg_t[:, c, 1:2],
                                            u_dst[:, c, cs:ce], start=(c == 0), stop=(c == DC - 1))
                        ins.then_inc(pe.h, 1); pe.v += 1
                    on('pe', logit_mm)
                    lg_v = pe.v
                    l1_r = rows[0:1, 4 * LC:5 * LC]
                    d_r = rows[0:1, 5 * LC:6 * LC]
                    mm0_r = rows[0:1, 6 * LC:7 * LC]
                    mm1_r = rows[0:1, 7 * LC:8 * LC]

                    def maskrows(ve):
                        W(ve, 'dv', pe, lg_v)
                        gwait(ve, 'dv', 'lnmrows')
                        ve.tensor_copy(l1_r, PS[5][0:1, :LC]).then_inc(dv.h, 1)
                        dv.v += 1
                        W(ve, 'dv', dv, dv.v)
                        ve.tensor_sub(d_r, PS[4][0:1, :LC], l1_r).then_inc(dv.h, 1)
                        dv.v += 1
                        W(ve, 'dv', dv, dv.v)
                        ve.tensor_single_scalar(mm0_r, d_r, 0.0, ALU.is_ge).then_inc(dv.h, 1)
                        dv.v += 1
                        W(ve, 'dv', dv, dv.v)
                        ve.tensor_scalar(mm1_r, mm0_r, -0.5, 0.5,
                                         ALU.mult, ALU.add).then_inc(dv.h, 1)
                        dv.v += 1
                        W(ve, 'dv', dv, dv.v)
                        ve.tensor_scalar_mul(mm0_r, mm0_r, 0.5).then_inc(dv.h, 1)
                        dv.v += 1
                    on('dv', maskrows)
                    gset(('ps', 4), dv, dv.v)
                    gset(('ps', 5), dv, dv.v)
                    mr_v = dv.v

                    def mask_bcast(te):
                        W(te, 'pe', dv, mr_v)
                        gwait(te, 'pe', ('ps', 6))
                        gwait(te, 'pe', ('ps', 7))
                        te.matmul(PS[6][:, :LC], ones1[:], mm0_r,
                                  start=True, stop=True).then_inc(pe.h, 1)
                        te.matmul(PS[7][:, :LC], ones1[:], mm1_r,
                                  start=True, stop=True).then_inc(pe.h, 1)
                        pe.v += 2
                    on('pe', mask_bcast)
                    mb_v = pe.v
                    gset('lnmrows', pe, mb_v)

                    def mask_copy(ve, cs=cs, ce=ce):
                        W(ve, 'dv', pe, mb_v)
                        ve.tensor_copy(m0f[:, cs:ce], PS[6][:, :LC]).then_inc(dv.h, 1)
                        ve.tensor_copy(m0bf[:, cs:ce], PS[6][:, :LC]).then_inc(dv.h, 1)
                        ve.tensor_copy(m1f[:, cs:ce], PS[7][:, :LC]).then_inc(dv.h, 1)
                        ve.tensor_copy(m1bf[:, cs:ce], PS[7][:, :LC]).then_inc(dv.h, 1)
                        dv.v += 4
                    on('dv', mask_copy)
                    gset(('ps', 6), dv, dv.v)
                    gset(('ps', 7), dv, dv.v)
            gset('lnscr', dv, dv.v)
            u_done = dv.v
            close_scope(esL)
            return u_done

        def emit_ffn(Xdst, u_ready, masks, tag, u_src):
            m0bf, m1bf, m0f, m1f = masks
            esF = ExitStack()
            h_sb = esF.enter_context(nc.sbuf_tensor(f"h{tag}", [128, FC * T], BF16))
            scr2 = esF.enter_context(nc.sbuf_tensor(f"s2{tag}", [128, 512], F32))
            SUPA = 256
            NSA = F // SUPA
            xadd = {}
            for ei, (Wa_r, Wb_r, ba, bb, fn, mbf, mf) in enumerate((
                    (W0a_r, W0b_r, b0a_t, b0b_t, AF.Gelu, m0bf, m0f),
                    (W1a_r, W1b_r, b1a_t, b1b_t, AF.Relu, m1bf, m1f))):
                loadv = {}

                def emit_load_a(sj, Wa_r=Wa_r, loadv=loadv):
                    def body(gp, sj=sj):
                        gwait(gp, 'pl', ('arena', sj % 2))
                        w_ = pw[sj % 2]
                        hc = DC // 2
                        gp.dma_start(out=wa[sj % 2][:, :hc * SUPA],
                                     in_=Wa_r[:, :hc, sj * SUPA:(sj + 1) * SUPA]
                                     ).then_inc(w_.h, 16)
                        gp.dma_start(out=wa[sj % 2][:, hc * SUPA:DC * SUPA],
                                     in_=Wa_r[:, hc:, sj * SUPA:(sj + 1) * SUPA]
                                     ).then_inc(w_.h, 16)
                        w_.v += 32
                        loadv[sj] = (w_, w_.v)
                    on('pl', body)
                emit_load_a(0)
                if NSA > 1:
                    emit_load_a(1)
                for sj in range(NSA):
                    last_mm = 0
                    for fo in range(SUPA // 128):
                        f = sj * (SUPA // 128) + fo
                        for tc in range(NTC):
                            cs, ce = tc * TC, (tc + 1) * TC
                            slot = (f * NTC + tc) % 2

                            def mm_body(te, sj=sj, fo=fo, cs=cs, ce=ce, slot=slot):
                                W(te, 'pe', *loadv[sj])
                                W(te, 'pe', dv, u_ready)
                                gwait(te, 'pe', ('ps', slot))
                                for c in range(DC):
                                    ins = te.matmul(
                                        PS[slot][:, :TC],
                                        wa[sj % 2][:, c * SUPA + fo * 128: c * SUPA + (fo + 1) * 128],
                                        u_src[:, c, cs:ce],
                                        start=(c == 0), stop=(c == DC - 1))
                                ins.then_inc(pe.h, 1); pe.v += 1
                            on('pe', mm_body)
                            mm_v = pe.v
                            last_mm = mm_v

                            def act_body(sc, f=f, cs=cs, ce=ce, slot=slot, mm_v=mm_v, fn=fn, ba=ba):
                                W(sc, 'ac', pe, mm_v)
                                W(sc, 'ac', hd, consts_hd)
                                gwait(sc, 'ac', ('hsb',))
                                sc.activation(out=h_sb[:, f * T + cs: f * T + ce],
                                              in_=PS[slot][:, :TC], func=fn,
                                              bias=ba[:, f:f + 1]).then_inc(ac.h, 1)
                                ac.v += 1
                            on('ac', act_body)
                            gset(('ps', slot), ac, ac.v)
                            act_v = ac.v

                            def mask_body(ve, f=f, cs=cs, ce=ce, act_v=act_v, mbf=mbf):
                                W(ve, 'dv', ac, act_v)
                                ve.tensor_mul(h_sb[:, f * T + cs: f * T + ce],
                                              h_sb[:, f * T + cs: f * T + ce],
                                              mbf[:, cs:ce]).then_inc(dv.h, 1)
                                dv.v += 1
                            on('dv', mask_body)
                    gset(('arena', sj % 2), pe, last_mm)
                    if sj + 2 < NSA:
                        emit_load_a(sj + 2)
                h_ready = dv.v
                SUPB = 128
                NSB = D // SUPB
                loadvb = {}

                def emit_load_b(es_, Wb_r=Wb_r, loadvb=loadvb):
                    def body(gp, es_=es_):
                        gwait(gp, 'pl', ('arena', es_ % 2))
                        w_ = pw[es_ % 2]
                        hc = FC // 2
                        gp.dma_start(out=wa[es_ % 2][:, :hc * SUPB],
                                     in_=Wb_r[:, :hc, es_ * SUPB:(es_ + 1) * SUPB]
                                     ).then_inc(w_.h, 16)
                        gp.dma_start(out=wa[es_ % 2][:, hc * SUPB:FC * SUPB],
                                     in_=Wb_r[:, hc:, es_ * SUPB:(es_ + 1) * SUPB]
                                     ).then_inc(w_.h, 16)
                        w_.v += 32
                        loadvb[es_] = (w_, w_.v)
                    on('pl', body)
                emit_load_b(0)
                if NSB > 1:
                    emit_load_b(1)
                for es_ in range(NSB):
                    e = es_
                    last_mm = 0
                    for tc in range(NTC):
                        cs, ce = tc * TC, (tc + 1) * TC
                        slot = 2 + (e * NTC + tc) % 2

                        def mmb(te, es_=es_, cs=cs, ce=ce, slot=slot, h_ready=h_ready):
                            W(te, 'pe', *loadvb[es_])
                            W(te, 'pe', dv, h_ready)
                            gwait(te, 'pe', ('ps', slot))
                            for j in range(FC):
                                ins = te.matmul(PS[slot][:, :TC],
                                                wa[es_ % 2][:, j * SUPB:(j + 1) * SUPB],
                                                h_sb[:, j * T + cs: j * T + ce],
                                                start=(j == 0), stop=(j == FC - 1))
                            ins.then_inc(pe.h, 1); pe.v += 1
                        on('pe', mmb)
                        mm_v = pe.v
                        last_mm = mm_v

                        def evac(ve, e=e, cs=cs, ce=ce, slot=slot, mm_v=mm_v, mf=mf, bb=bb):
                            W(ve, 'dv', pe, mm_v)
                            W(ve, 'dv', hd, consts_hd)
                            gwait(ve, 'dv', 'scr2')
                            ve.scalar_tensor_tensor(scr2[:, :TC], mf[:, cs:ce],
                                                    bb[:, e:e + 1], PS[slot][:, :TC],
                                                    ALU.mult, ALU.add).then_inc(dv.h, 1)
                            dv.v += 1
                            W(ve, 'dv', dv, dv.v)
                            if (e, cs) in xadd:
                                W(ve, 'dv', dv, xadd[(e, cs)])
                            ve.tensor_add(Xdst[:, e, cs:ce], Xdst[:, e, cs:ce],
                                          scr2[:, :TC]).then_inc(dv.h, 1)
                            dv.v += 1
                            xadd[(e, cs)] = dv.v
                            gset('scr2', dv, dv.v)
                        on('dv', evac)
                        gset(('ps', slot), dv, dv.v)
                    gset(('arena', es_ % 2), pe, last_mm)
                    if es_ + 2 < NSB:
                        emit_load_b(es_ + 2)
                gset(('hsb',), pe, last_mm)
            gset('uarena', pe, pe.v)
            ffn_done = dv.v
            close_scope(esF)
            return ffn_done

        esM1 = ExitStack()
        masks1 = tuple(esM1.enter_context(nc.sbuf_tensor(n, [128, T], d))
                       for n, d in (("m0bf", BF16), ("m1bf", BF16),
                                    ("m0f", F32), ("m1f", F32)))
        u1_v = emit_ln(X, masks1, (hd, x_ready), u_t)
        emit_ffn(X, u1_v, masks1, "f1", u_t)
        close_scope(esM1)

        u2_v = emit_ln(X, None, (dv, dv.v), u_t)

        def spill(sp):
            W(sp, 'sp', dv, u2_v)
            sp.dma_start(out=xsp_r, in_=X[:, :, :]).then_inc(hsp.h, 16)
            hsp.v += 16
        on('sp', spill)
        close_scope(esX)

        esQ = ExitStack()
        qs = esQ.enter_context(nc.sbuf_tensor("qs", [128, DC, T], BF16, side="right"))
        esB_ = ExitStack()
        sq = esB_.enter_context(nc.sbuf_tensor("sq", [128, DC, T], BF16, side="right"))
        raws = esB_.enter_context(nc.sbuf_tensor("raws", [1, M * T], BF16, side="right"))
        scr2b = esB_.enter_context(nc.sbuf_tensor("scr2b", [128, 512], BF16, side="right"))
        esA_ = ExitStack()
        kvst = esA_.enter_context(nc.sbuf_tensor("kvst", [128, DC * T], BF16, side="right"))
        inv_sqrt_hd = float(1.0 / np.sqrt(128.0))

        SUPQ = 256
        NSQ = D // SUPQ

        def emit_proj(Wh_r, kind):
            loadv = {}

            def emit_load(os_):
                def body(gp, os_=os_):
                    gwait(gp, 'pl', ('arena', os_ % 2))
                    w_ = pw[os_ % 2]
                    hc = DC // 2
                    gp.dma_start(out=wa[os_ % 2][:, :hc * SUPQ],
                                 in_=Wh_r[:, :hc, os_ * SUPQ:(os_ + 1) * SUPQ]
                                 ).then_inc(w_.h, 16)
                    gp.dma_start(out=wa[os_ % 2][:, hc * SUPQ:DC * SUPQ],
                                 in_=Wh_r[:, hc:, os_ * SUPQ:(os_ + 1) * SUPQ]
                                 ).then_inc(w_.h, 16)
                    w_.v += 32
                    loadv[os_] = (w_, w_.v)
                on('pl', body)
            emit_load(0)
            if NSQ > 1:
                emit_load(1)
            for os_ in range(NSQ):
                last_mm = 0
                if kind != 'v':
                    for ko in range(SUPQ // 128):
                        k = os_ * (SUPQ // 128) + ko
                        for tc in range(NTC):
                            cs, ce = tc * TC, (tc + 1) * TC
                            slot = (k * NTC + tc) % 2

                            def mmq(te, os_=os_, ko=ko, cs=cs, ce=ce, slot=slot):
                                W(te, 'pe', *loadv[os_])
                                W(te, 'pe', dv, u2_v)
                                gwait(te, 'pe', ('ps', slot))
                                for c in range(DC):
                                    ins = te.matmul(
                                        PS[slot][:, :TC],
                                        wa[os_ % 2][:, c * SUPQ + ko * 128: c * SUPQ + (ko + 1) * 128],
                                        u_t[:, c, cs:ce],
                                        start=(c == 0), stop=(c == DC - 1))
                                ins.then_inc(pe.h, 1); pe.v += 1
                            on('pe', mmq)
                            mm_v = pe.v
                            last_mm = mm_v
                            if kind == 'q':
                                def evq(sc, k=k, cs=cs, ce=ce, slot=slot, mm_v=mm_v):
                                    W(sc, 'ac', pe, mm_v)
                                    gwait(sc, 'ac', 'scr2b')
                                    gwait(sc, 'ac', 'scrfq')
                                    sc.activation(out=qs[:, k, cs:ce], in_=PS[slot][:, :TC],
                                                  func=AF.Copy, scale=inv_sqrt_hd)
                                    sc.activation(out=sq[:, k, cs:ce], in_=PS[slot][:, :TC],
                                                  func=AF.Relu)
                                    sc.activation(out=scrf[:, 512:512 + TC],
                                                  in_=PS[slot][:, :TC],
                                                  func=AF.Relu, scale=-1.0).then_inc(ac.h, 1)
                                    ac.v += 1
                                    W(sc, 'ac', ac, ac.v)
                                    sc.activation(out=scr2b[:, :TC],
                                                  in_=scrf[:, 512:512 + TC],
                                                  func=AF.Exp, scale=-1.0).then_inc(ac.h, 1)
                                    ac.v += 1
                                on('ac', evq)
                                gset(('ps', slot), ac, ac.v)
                                gset('scrfq', ac, ac.v)
                                e_v = ac.v

                                def addsig(ve, k=k, cs=cs, ce=ce, e_v=e_v):
                                    W(ve, 'dv', ac, e_v)
                                    ve.tensor_add(sq[:, k, cs:ce], sq[:, k, cs:ce],
                                                  scr2b[:, :TC]).then_inc(dv.h, 1)
                                    dv.v += 1
                                    gset('scr2b', dv, dv.v)
                                on('dv', addsig)
                            else:
                                def evk(sc, k=k, cs=cs, ce=ce, slot=slot, mm_v=mm_v):
                                    W(sc, 'ac', pe, mm_v)
                                    gwait(sc, 'ac', ('kvst',))
                                    sc.activation(out=kvst[:, k * T + cs: k * T + ce],
                                                  in_=PS[slot][:, :TC],
                                                  func=AF.Copy).then_inc(ac.h, 1)
                                    ac.v += 1
                                on('ac', evk)
                                gset(('ps', slot), ac, ac.v)
                else:
                    for tt in range(NOB):
                        slot = tt % 2

                        def mmv(te, os_=os_, tt=tt, slot=slot):
                            W(te, 'pe', *loadv[os_])
                            W(te, 'pe', dv, u2_v)
                            gwait(te, 'pe', ('ps', slot))
                            for c in range(DC):
                                ins = te.matmul(PS[slot][:, :SUPQ],
                                                u_t[:, c, tt * 128:(tt + 1) * 128],
                                                wa[os_ % 2][:, c * SUPQ:(c + 1) * SUPQ],
                                                start=(c == 0), stop=(c == DC - 1))
                            ins.then_inc(pe.h, 1); pe.v += 1
                        on('pe', mmv)
                        mm_v = pe.v
                        last_mm = mm_v

                        def evv(sc, os_=os_, tt=tt, slot=slot, mm_v=mm_v):
                            W(sc, 'ac', pe, mm_v)
                            gwait(sc, 'ac', ('kvst',))
                            sc.activation(out=kvst[:, tt * D + os_ * SUPQ: tt * D + (os_ + 1) * SUPQ],
                                          in_=PS[slot][:, :SUPQ],
                                          func=AF.Copy).then_inc(ac.h, 1)
                            ac.v += 1
                        on('ac', evv)
                        gset(('ps', slot), ac, ac.v)
                gset(('arena', os_ % 2), pe, last_mm)
                if os_ + 2 < NSQ:
                    emit_load(os_ + 2)

        emit_proj(wq_r, 'q')
        sig_done = dv.v
        q_done = ac.v
        emit_proj(wk_r, 'k')
        k_done = ac.v

        def send_k(sp):
            W(sp, 'sp', ac, k_done)
            sp.dma_start(out=kin_r,
                         in_=kvst[:].rearrange("p (c t) -> p c t", c=DC)
                         ).then_inc(hk.h, 16)
            hk.v += 16
        on('sp', send_k)
        gset(('kvst',), hk, hk.v)
        kin_v = hk.v

        emit_proj(wv_r, 'v')
        v_done = ac.v
        gset('uarena', pe, pe.v)

        def send_v(sp):
            W(sp, 'sp', ac, v_done)
            sp.dma_start(out=vin_r,
                         in_=kvst[:].rearrange("p (b d) -> p b d", b=NOB)
                         ).then_inc(hv.h, 16)
            hv.v += 16
        on('sp', send_v)
        vin_v = hv.v

        for m in range(M):
            for tc in range(NTC):
                cs, ce = tc * TC, (tc + 1) * TC
                slot = 4 + tc % 2

                def mmr(te, m=m, cs=cs, ce=ce, slot=slot):
                    W(te, 'pe', dv, sig_done)
                    W(te, 'pe', pd, consts_pd)
                    gwait(te, 'pe', ('ps', slot))
                    for c in range(DC):
                        ins = te.matmul(PS[slot][0:1, :TC], norms_t[:, c, m:m + 1],
                                        sq[:, c, cs:ce], start=(c == 0), stop=(c == DC - 1))
                    ins.then_inc(pe.h, 1); pe.v += 1
                on('pe', mmr)
                mm_v = pe.v

                def evr(ve, m=m, cs=cs, ce=ce, slot=slot, mm_v=mm_v, tc=tc):
                    W(ve, 'dv', pe, mm_v)
                    ve.tensor_copy(raws[0:1, m * T + cs: m * T + ce],
                                   PS[slot][0:1, :TC]).then_inc(dv.h, 1)
                    ve.reduce_sum(srow2[0:1, m * NTC + tc: m * NTC + tc + 1],
                                  PS[slot][0:1, :TC], axis=AX.X).then_inc(dv.h, 1)
                    dv.v += 2
                on('dv', evr)
                gset(('ps', slot), dv, dv.v)

        def relsum(ve):
            W(ve, 'dv', dv, dv.v)
            ve.memset(smin_t[0:1, 0:8], 0.0).then_inc(dv.h, 1)
            dv.v += 1
            W(ve, 'dv', dv, dv.v)
            if NTC == 2:
                ve.tensor_add(smin_t[0:1, 0:M],
                              srow2[0:1, 0:2 * M:2],
                              srow2[0:1, 1:2 * M:2]).then_inc(dv.h, 1)
            else:
                ve.tensor_copy(smin_t[0:1, 0:M], srow2[0:1, 0:M]).then_inc(dv.h, 1)
            dv.v += 1
        on('dv', relsum)
        rel_v = dv.v

        def send_sm(sp):
            W(sp, 'sp', dv, rel_v)
            sp.dma_start(out=smin.ap(), in_=smin_t[:]).then_inc(hsn.h, 16)
            hsn.v += 16
        on('sp', send_sm)
        sm_v = hsn.v

        def colls(gp):
            W(gp, 'pl', hsn, sm_v)
            gp.collective_compute("AllGather", ALU.bypass,
                                  replica_groups=[[0, 1], [2, 3], [4, 5], [6, 7]],
                                  ins=[smin.ap()], outs=[smg.ap()]).then_inc(cc2.h, 1)
            cc2.v += 1
            W(gp, 'pl', hk, kin_v)
            gp.collective_compute("AllGather", ALU.bypass,
                                  replica_groups=[[0, 1], [2, 3], [4, 5], [6, 7]],
                                  ins=[kin.ap()], outs=[kg.ap()]).then_inc(cc1.h, 1)
            cc1.v += 1
            W(gp, 'pl', hv, vin_v)
            gp.collective_compute("AllGather", ALU.bypass,
                                  replica_groups=[[0, 1], [2, 3], [4, 5], [6, 7]],
                                  ins=[vin.ap()], outs=[vg.ap()]).then_inc(cc1.h, 1)
            cc1.v += 1
        on('pl', colls)
        close_scope(esA_)
        close_scope(esU1)

        esP = ExitStack()
        pre = esP.enter_context(nc.sbuf_tensor("pre", [128, DC, T], F32))
        esS = ExitStack()
        kslab = [esS.enter_context(nc.sbuf_tensor(f"ks{o}", [128, 2, T], BF16, side="right"))
                 for o in range(2)]
        vslab = [esS.enter_context(nc.sbuf_tensor(f"vs{o}", [128, 2, NOB, 128], BF16, side="right"))
                 for o in range(2)]
        attnw = esS.enter_context(nc.sbuf_tensor("attnw", [128, 2, 4, 512], BF16, side="right"))
        lrow = esS.enter_context(nc.sbuf_tensor("lrow", [1, T], F32, side="right"))
        cbc = [esS.enter_context(nc.sbuf_tensor(f"cbc{i}", [128, T], BF16, side="right"))
               for i in range(2)]
        mscr = esS.enter_context(nc.sbuf_tensor("mscr", [128, 2, 512], F32, side="right"))
        ocol = esS.enter_context(nc.sbuf_tensor("ocol", [128, 1], F32, side="right"))
        wcol = esS.enter_context(nc.sbuf_tensor("wcol", [128, 4], F32, side="right"))

        def smrecv(sp):
            W(sp, 'sp', cc2, 1)
            sp.dma_start(out=smg_t0[:], in_=smg.ap()[0:1, :]).then_inc(hsm2.h, 16)
            sp.dma_start(out=smg_t1[:], in_=smg.ap()[1:2, :]).then_inc(hsm2.h, 16)
            hsm2.v += 32
        on('sp', smrecv)
        smg_v = hsm2.v

        def relcomb(ve):
            W(ve, 'dv', hsm2, smg_v)
            ve.tensor_add(srow[0:1, 0:M], smg_t0[0:1, 0:M],
                          smg_t1[0:1, 0:M]).then_inc(dv.h, 1)
            dv.v += 1
            W(ve, 'dv', dv, dv.v)
            ve.tensor_scalar_mul(srow[0:1, 0:M], srow[0:1, 0:M],
                                 1.0 / cfg.S).then_inc(dv.h, 1)
            dv.v += 1
            W(ve, 'dv', dv, dv.v)
            ve.reduce_max(srow[0:1, 8:9], srow[0:1, 0:M], axis=AX.X).then_inc(dv.h, 1)
            dv.v += 1
            W(ve, 'dv', dv, dv.v)
            ve.tensor_scalar_mul(srow[0:1, 9:10], srow[0:1, 8:9], -1.0).then_inc(dv.h, 1)
            dv.v += 1
        on('dv', relcomb)
        rr_v = dv.v

        def relexp(sc):
            W(sc, 'ac', dv, rr_v)
            W(sc, 'ac', hd, consts_hd)
            sc.activation(out=srow[0:1, 4:4 + M], in_=srow[0:1, 0:M],
                          func=AF.Exp, bias=srow[0:1, 9:10]).then_inc(ac.h, 1)
            sc.activation(out=gsig[:], in_=gate_t[:],
                          func=AF.Sigmoid).then_inc(ac.h, 1)
            ac.v += 2
        on('ac', relexp)
        re_v = ac.v

        def relw(ve):
            W(ve, 'dv', ac, re_v)
            ve.reduce_sum(srow[0:1, 10:11], srow[0:1, 4:4 + M], axis=AX.X).then_inc(dv.h, 1)
            dv.v += 1
            W(ve, 'dv', dv, dv.v)
            ve.reciprocal(srow[0:1, 11:12], srow[0:1, 10:11]).then_inc(dv.h, 1)
            dv.v += 1
            W(ve, 'dv', dv, dv.v)
            ve.tensor_scalar_mul(srow[0:1, 12:12 + M], srow[0:1, 4:4 + M],
                                 srow[0:1, 11:12]).then_inc(dv.h, 1)
            dv.v += 1
            W(ve, 'dv', dv, dv.v)
            ve.tensor_scalar_mul(srow[0:1, 12:12 + M], srow[0:1, 12:12 + M],
                                 gsig[0:1, 0:1]).then_inc(dv.h, 1)
            dv.v += 1
            W(ve, 'dv', dv, dv.v)
            ve.tensor_scalar(omg[:], gsig[:], -1.0, 1.0, ALU.mult, ALU.add).then_inc(dv.h, 1)
            dv.v += 1
        on('dv', relw)
        gw_v = dv.v

        def cbcol(te):
            W(te, 'pe', dv, gw_v)
            W(te, 'pe', pl, consts_pl)
            gwait(te, 'pe', ('ps', 6))
            te.matmul(PS[6][:, 0:1], ones1[:], omg[0:1, 0:1],
                      start=True, stop=True)
            te.matmul(PS[6][:, 1:5], ones1[:], srow[0:1, 12:12 + M],
                      start=True, stop=True).then_inc(pe.h, 1)
            pe.v += 1
        on('pe', cbcol)
        cbcol_v = pe.v

        def cbcol2(ve):
            W(ve, 'dv', pe, cbcol_v)
            ve.tensor_copy(ocol[:, :], PS[6][:, 0:1]).then_inc(dv.h, 1)
            ve.tensor_copy(wcol[:, :], PS[6][:, 1:5]).then_inc(dv.h, 1)
            dv.v += 2
        on('dv', cbcol2)
        gset(('ps', 6), dv, dv.v)

        SUPM = 256
        NSM = D // SUPM
        pre_w = {}
        mem_pairs = [(m, es_) for m in range(M) for es_ in range(NSM)]
        mem_loadv = {}

        def emit_load_m(p_i):
            m_, e_ = mem_pairs[p_i]

            def body(gp, p_i=p_i, m_=m_, e_=e_):
                gwait(gp, 'pl', ('arena', p_i % 2))
                w_ = pw[p_i % 2]
                hc = DC // 2
                gp.dma_start(out=wa[p_i % 2][:, :hc * SUPM],
                             in_=mems_r[m_, :, :hc, e_ * SUPM:(e_ + 1) * SUPM]
                             ).then_inc(w_.h, 16)
                gp.dma_start(out=wa[p_i % 2][:, hc * SUPM:DC * SUPM],
                             in_=mems_r[m_, :, hc:, e_ * SUPM:(e_ + 1) * SUPM]
                             ).then_inc(w_.h, 16)
                w_.v += 32
                mem_loadv[p_i] = (w_, w_.v)
            on('pl', body)

        mchain_i = [0]

        def memq_gen():
            emit_load_m(0)
            emit_load_m(1)
            for p_i, (m, es_) in enumerate(mem_pairs):
                if es_ == 0:
                    def crow(ve, m=m):
                        W(ve, 'dv', dv, gw_v)
                        es_lp = nc.allow_low_precision(reason="bf16 in-place clamp of raw row (reference-tolerant path)")
                        es_lp.__enter__()
                        ve.tensor_scalar_max(raws[0:1, m * T:(m + 1) * T],
                                             raws[0:1, m * T:(m + 1) * T], 1e-6).then_inc(dv.h, 1)
                        dv.v += 1
                        es_lp.__exit__(None, None, None)
                    yield ('crow', m, crow)
                    for tc2 in range(NTC):
                        def cbcast(te, m=m, tc2=tc2):
                            W(te, 'pe', dv, dv.v)
                            W(te, 'pe', pl, consts_pl)
                            gwait(te, 'pe', ('ps', 6))
                            te.matmul(PS[6][:, :TC], ones1b[:],
                                      raws[0:1, m * T + tc2 * TC: m * T + (tc2 + 1) * TC],
                                      start=True, stop=True).then_inc(pe.h, 1)
                            pe.v += 1
                        yield ('cbcast', m, tc2, cbcast)
                for eo in range(SUPM // 128):
                    e = es_ * (SUPM // 128) + eo
                    for tc2 in range(NTC):
                        cs, ce = tc2 * TC, (tc2 + 1) * TC
                        slot = 6 + mchain_i[0] % 2
                        mchain_i[0] += 1

                        def chain(te, p_i=p_i, eo=eo, cs=cs, ce=ce, slot=slot):
                            W(te, 'pe', *mem_loadv[p_i])
                            W(te, 'pe', dv, sig_done)
                            gwait(te, 'pe', ('ps', slot))
                            for c in range(DC):
                                ins = te.matmul(
                                    PS[slot][:, :TC],
                                    wa[p_i % 2][:, c * SUPM + eo * 128: c * SUPM + (eo + 1) * 128],
                                    sq[:, c, cs:ce],
                                    start=(c == 0), stop=(c == DC - 1))
                            ins.then_inc(pe.h, 1); pe.v += 1

                        def evm(ve, m=m, e=e, cs=cs, ce=ce, slot=slot):
                            half = slot - 6
                            ve.tensor_mul(mscr[:, half, :], PS[slot][:, :TC],
                                          cbc[m % 2][:, cs:ce]).then_inc(dv.h, 1)
                            dv.v += 1
                            W(ve, 'dv', dv, dv.v)
                            if (e, cs) in pre_w:
                                W(ve, 'dv', *pre_w[(e, cs)])
                            if m == 0:
                                ve.tensor_copy(pre[:, e, cs:ce], mscr[:, half, :]).then_inc(dv.h, 1)
                            else:
                                ve.tensor_add(pre[:, e, cs:ce], pre[:, e, cs:ce],
                                              mscr[:, half, :]).then_inc(dv.h, 1)
                            dv.v += 1
                            pre_w[(e, cs)] = (dv, dv.v)
                        yield ('chain', p_i, chain, evm, slot)
                if p_i + 2 < len(mem_pairs):
                    emit_load_m(p_i + 2)

        memq = memq_gen()
        mem_done = [False]

        def emit_one_memq():
            try:
                item = next(memq)
            except StopIteration:
                mem_done[0] = True
                return
            if item[0] == 'crow':
                on('dv', item[2])
            elif item[0] == 'cbcast':
                _, m, tc2, f = item
                on('pe', f)
                cb_v = pe.v

                def cbrec(ve, m=m, tc2=tc2, cb_v=cb_v):
                    W(ve, 'dv', pe, cb_v)
                    ve.reciprocal(scrf[:, 0:TC], PS[6][:, :TC]).then_inc(dv.h, 1)
                    dv.v += 1
                    W(ve, 'dv', dv, dv.v)
                    ve.tensor_scalar_mul(cbc[m % 2][:, tc2 * TC:(tc2 + 1) * TC],
                                         scrf[:, 0:TC],
                                         wcol[:, m:m + 1]).then_inc(dv.h, 1)
                    dv.v += 1
                on('dv', cbrec)
                gset(('ps', 6), dv, dv.v)
            else:
                _, p_i, chain, evm, slot = item
                on('pe', chain)
                mm_v = pe.v

                def evm_w(ve, evm=evm, mm_v=mm_v):
                    W(ve, 'dv', pe, mm_v)
                    evm(ve)
                on('dv', evm_w)
                gset(('ps', slot), dv, dv.v - 1)
                gset(('arena', p_i % 2), pe, mm_v)

        N_MEMQ = M * (3 + NSM * (SUPM // 128) * NTC)
        FRONT = 40
        for _ in range(FRONT):
            emit_one_memq()
        assert len(pre_w) >= DC * NTC, "front-load must cover memory 0 fully"

        def parts_of(tc):
            out = []
            cs0, ce0 = tc * TC, (tc + 1) * TC
            for j in range(NB):
                a = max(cs0, ustart_blk[j] * 128)
                if a < ce0:
                    out.append((j, a, ce0))
            return out

        packed = {}
        for tc in range(NTC):
            parts = parts_of(tc)
            slices = []
            cur, cur_w = [], 0
            for (j, a, b) in parts:
                w = b - a
                if cur_w + w > 512:
                    slices.append(cur); cur, cur_w = [], 0
                cur.append((j, a, b, cur_w)); cur_w += w
            if cur:
                slices.append(cur)
            groups = [slices[i:i + 4] for i in range(0, len(slices), 4)]
            packed[tc] = (parts, groups)

        aw_guard = {}
        memq_budget = [0.0]
        n_groups_total = DC * sum(len(packed[tc][1]) for tc in range(NTC))
        memq_per_group = float(N_MEMQ - FRONT) / max(1, n_groups_total)

        slab_v = 0
        slab_hist = {}
        for h in range(DC):
            par = h % 2

            hsl_p = hsl if par == 0 else hsl2

            def slabload(sp, h=h, par=par, hsl_p=hsl_p):
                W(sp, 'sp', cc1, 2)
                gwait(sp, 'sp', ('slab', par))
                for o in range(2):
                    sp.dma_start(out=kslab[o][:, par, :],
                                 in_=kg_r[:, o * DC + h, :]).then_inc(hsl_p.h, 16)
                    sp.dma_start(out=vslab[o][:, par, :, :],
                                 in_=vg_r[:, o * NOB:(o + 1) * NOB,
                                          h * 128:(h + 1) * 128]).then_inc(hsl_p.h, 16)
                    hsl_p.v += 32
            on('sp', slabload)
            slab_v = hsl_p.v
            slab_last = 0
            for tc in range(NTC):
                parts, groups = packed[tc]
                first_j = parts[0][0]
                last_j = parts[-1][0]
                for gi, grp in enumerate(groups):
                    n_sl = len(grp)
                    def scores(te, h=h, par=par, grp=grp, slab_v=slab_v, hsl_p=hsl_p):
                        W(te, 'pe', hsl_p, slab_v)
                        W(te, 'pe', ac, q_done)
                        gwait(te, 'pe', ('sc',))
                        for bi in range(4):
                            gwait(te, 'pe', ('ps', bi))
                        for si, sl in enumerate(grp):
                            for (j, a, b, off) in sl:
                                o = owner_of[j]
                                pj = pos_of[j]
                                ins = te.matmul(SCP[:, si, off:off + b - a],
                                                kslab[o][:, par, pj * 128:(pj + 1) * 128],
                                                qs[:, h, a:b],
                                                start=True, stop=True)
                        ins.then_inc(pe.h, 1); pe.v += 1
                    on('pe', scores)
                    sc_v = pe.v
                    slab_last = sc_v
                    memq_budget[0] += memq_per_group
                    while memq_budget[0] >= 1.0 and not mem_done[0]:
                        emit_one_memq()
                        memq_budget[0] -= 1.0

                    half = (h * 16 + tc * 8 + gi) % 2

                    def expg(sc_, sc_v=sc_v, half=half, n_sl=n_sl):
                        W(sc_, 'ac', pe, sc_v)
                        W(sc_, 'ac', pl, consts_pl)
                        gwait(sc_, 'ac', ('aw', half))
                        sc_.activation(
                            out=attnw[:, half, 0:n_sl, :].rearrange("p a b -> p (a b)"),
                            in_=SCP[:, 0:n_sl, :].rearrange("p a b -> p (a b)"),
                            func=AF.Exp, bias=shift_t[:, 0:1]).then_inc(ac.h, 1)
                        ac.v += 1
                    on('ac', expg)
                    exp_v = ac.v
                    gset(('sc',), ac, exp_v)

                    mask_ops = []
                    for si, sl in enumerate(grp):
                        for (j, a, b, off) in sl:
                            ms = ustart_blk[j] * 128
                            me = min(ms + 256, T)
                            x0, x1 = max(a, ms), min(b, me)
                            if x0 < x1:
                                mask_ops.append((si, j, a, off, x0, x1, ms))
                    mask_v = 0
                    if mask_ops:
                        def maskg(ve, mask_ops=mask_ops, half=half, exp_v=exp_v):
                            W(ve, 'dv', ac, exp_v)
                            W(ve, 'dv', hd, consts_hd)
                            for (si, j, a, off, x0, x1, ms) in mask_ops:
                                ins = ve.tensor_mul(
                                    attnw[:, half, si, off + x0 - a: off + x1 - a],
                                    attnw[:, half, si, off + x0 - a: off + x1 - a],
                                    amask_t[:, j, x0 - ms: x1 - ms])
                            ins.then_inc(dv.h, 1); dv.v += 1
                        on('dv', maskg)
                        mask_v = dv.v

                    def lav(te, h=h, tc=tc, par=par, grp=grp, half=half, exp_v=exp_v,
                            mask_v=mask_v, first_j=first_j, last_j=last_j):
                        W(te, 'pe', ac, exp_v)
                        if mask_v:
                            W(te, 'pe', dv, mask_v)
                        gwait(te, 'pe', ('ps', 4))
                        gwait(te, 'pe', ('ps', 5))
                        co = tc * TC
                        for si, sl in enumerate(grp):
                            for (j, a, b, off) in sl:
                                o = owner_of[j]
                                pj = pos_of[j]
                                st = (j == first_j)
                                sp_ = (j == last_j)
                                te.matmul(PS[5][0:1, a - co: b - co],
                                          onescol[:],
                                          attnw[:, half, si, off:off + b - a],
                                          start=st, stop=sp_)
                                ins = te.matmul(PS[4][:, a - co: b - co],
                                                vslab[o][:, par, pj, :],
                                                attnw[:, half, si, off:off + b - a],
                                                start=st, stop=sp_)
                        ins.then_inc(pe.h, 1); pe.v += 1
                    on('pe', lav)
                    aw_guard[half] = pe.v
                    gset(('aw', half), pe, pe.v)
                    slab_last = pe.v
                lav_v = pe.v

                def lep(ve, tc=tc, lav_v=lav_v):
                    W(ve, 'dv', pe, lav_v)
                    ve.tensor_copy(lrow[0:1, tc * TC:(tc + 1) * TC],
                                   PS[5][0:1, :TC]).then_inc(dv.h, 1)
                    dv.v += 1
                on('dv', lep)
                lep_v = dv.v

                def lbc(te, tc=tc, lep_v=lep_v):
                    W(te, 'pe', dv, lep_v)
                    te.matmul(PS[5][:, :TC], ones1[:],
                              lrow[0:1, tc * TC:(tc + 1) * TC],
                              start=True, stop=True).then_inc(pe.h, 1)
                    pe.v += 1
                on('pe', lbc)
                lbc_v = pe.v

                def avev(ve, h=h, tc=tc, lbc_v=lbc_v):
                    cs, ce = tc * TC, (tc + 1) * TC
                    W(ve, 'dv', pe, lbc_v)
                    gwait(ve, 'dv', 'scrfq')
                    ve.reciprocal(scrf[:, 0:TC], PS[5][:, :TC]).then_inc(dv.h, 1)
                    dv.v += 1
                    W(ve, 'dv', dv, dv.v)
                    ve.tensor_mul(scrf[:, 512:512 + TC], PS[4][:, :TC],
                                  scrf[:, 0:TC]).then_inc(dv.h, 1)
                    dv.v += 1
                    W(ve, 'dv', dv, dv.v)
                    if (h, cs) in pre_w:
                        W(ve, 'dv', *pre_w[(h, cs)])
                    ve.scalar_tensor_tensor(pre[:, h, cs:ce], scrf[:, 512:512 + TC],
                                            ocol[:, 0:1], pre[:, h, cs:ce],
                                            ALU.mult, ALU.add).then_inc(dv.h, 1)
                    dv.v += 1
                    pre_w[(h, cs)] = (dv, dv.v)
                    gset('scrfq', dv, dv.v)
                on('dv', avev)
                gset(('ps', 4), dv, dv.v)
                gset(('ps', 5), dv, dv.v)
            slab_hist[h] = slab_last
            if h >= 1:
                gset(('slab', (h + 1) % 2), pe, slab_hist[h - 1])

        while not mem_done[0]:
            emit_one_memq()
        for bi in range(4):
            gset(('ps', bi), ac, ac.v)
        close_scope(esS)
        close_scope(esB_)
        close_scope(esQ)

        esX2 = ExitStack()
        X2 = esX2.enter_context(nc.sbuf_tensor("X2", [128, DC, T], F32, side="right"))
        esU2 = ExitStack()
        pb = esU2.enter_context(nc.sbuf_tensor("pb", [128, DC, T], BF16, side="right"))
        xrel = {}

        def xreload(sp):
            W(sp, 'sp', hsp, 16)
            half = DC // 2
            q4 = DC // 4
            for qi in range(2):
                sp.dma_start(out=X2[:, qi * q4:(qi + 1) * q4, :],
                             in_=xsp_r[:, qi * q4:(qi + 1) * q4, :]).then_inc(hx2[0].h, 16)
                hx2[0].v += 16
            for e in range(half):
                xrel[e] = (hx2[0], hx2[0].v)
            for qi in range(2, 4):
                sp.dma_start(out=X2[:, qi * q4:(qi + 1) * q4, :],
                             in_=xsp_r[:, qi * q4:(qi + 1) * q4, :]).then_inc(hx2[1].h, 16)
                hx2[1].v += 16
            for e in range(half, DC):
                xrel[e] = (hx2[1], hx2[1].v)
        on('sp', xreload)

        def prebf(ve):
            for c in range(DC):
                for tc in range(NTC):
                    cs, ce = tc * TC, (tc + 1) * TC
                    W(ve, 'dv', *pre_w[(c, cs)])
                    ve.tensor_copy(pb[:, c, cs:ce], pre[:, c, cs:ce]).then_inc(dv.h, 1)
                    dv.v += 1
        on('dv', prebf)
        prebf_v = dv.v

        SUPO = 256
        NSO = D // SUPO
        loadvo = {}

        def emit_load_o(os_):
            def body(gp, os_=os_):
                gwait(gp, 'pl', ('arena', os_ % 2))
                w_ = pw[os_ % 2]
                hc = DC // 2
                gp.dma_start(out=wa[os_ % 2][:, :hc * SUPO],
                             in_=wo_r[:, :hc, os_ * SUPO:(os_ + 1) * SUPO]
                             ).then_inc(w_.h, 16)
                gp.dma_start(out=wa[os_ % 2][:, hc * SUPO:DC * SUPO],
                             in_=wo_r[:, hc:, os_ * SUPO:(os_ + 1) * SUPO]
                             ).then_inc(w_.h, 16)
                w_.v += 32
                loadvo[os_] = (w_, w_.v)
            on('pl', body)
        emit_load_o(0)
        if NSO > 1:
            emit_load_o(1)
        for os_ in range(NSO):
            last_mm = 0
            for ko in range(SUPO // 128):
                k = os_ * (SUPO // 128) + ko
                for tc in range(NTC):
                    cs, ce = tc * TC, (tc + 1) * TC
                    slot = (k * NTC + tc) % 2

                    def mmo(te, os_=os_, ko=ko, cs=cs, ce=ce, slot=slot):
                        W(te, 'pe', *loadvo[os_])
                        W(te, 'pe', dv, prebf_v)
                        gwait(te, 'pe', ('ps', slot))
                        for c in range(DC):
                            ins = te.matmul(
                                PS[slot][:, :TC],
                                wa[os_ % 2][:, c * SUPO + ko * 128: c * SUPO + (ko + 1) * 128],
                                pb[:, c, cs:ce],
                                start=(c == 0), stop=(c == DC - 1))
                        ins.then_inc(pe.h, 1); pe.v += 1
                    on('pe', mmo)
                    mm_v = pe.v
                    last_mm = mm_v

                    def evo(ve, k=k, cs=cs, ce=ce, slot=slot, mm_v=mm_v):
                        W(ve, 'dv', pe, mm_v)
                        W(ve, 'dv', *xrel[k])
                        ve.tensor_add(X2[:, k, cs:ce], X2[:, k, cs:ce],
                                      PS[slot][:, :TC]).then_inc(dv.h, 1)
                        dv.v += 1
                    on('dv', evo)
                    gset(('ps', slot), dv, dv.v)
            gset(('arena', os_ % 2), pe, last_mm)
            if os_ + 2 < NSO:
                emit_load_o(os_ + 2)
        close_scope(esU2)
        close_scope(esP)
        guards.pop('uarena', None)
        gset('uarena', pe, pe.v)

        esU3 = ExitStack()
        u3_t = esU3.enter_context(nc.sbuf_tensor("ub", [128, DC, T], BF16))
        esM2 = ExitStack()
        masks2 = tuple(esM2.enter_context(nc.sbuf_tensor(n, [128, T], d))
                       for n, d in (("n0bf", BF16), ("n1bf", BF16),
                                    ("n0f", F32), ("n1f", F32)))
        u3_v = emit_ln(X2, masks2, (dv, dv.v), u3_t)
        ffn2_v = emit_ffn(X2, u3_v, masks2, "f2", u3_t)
        close_scope(esM2)
        close_scope(esU3)

        def store(sp):
            W(sp, 'sp', dv, ffn2_v)
            sp.dma_start(out=out_r, in_=X2[:, :, :]).then_inc(hout.h, 16)
            hout.v += 16
            sp.wait_ge(hout.h, hout.v)
        on('sp', store)
        esX2.close()
    return nc


def _host_amask(sub, cfg):
    NB, NOB = cfg.NB, cfg.NOB
    OB0, OB1 = own_blocks(0, NB), own_blocks(1, NB)
    myOB = OB0 if sub == 0 else OB1
    am = np.ones((128, NB, 256), np.float32)
    for j in range(NB):
        us = min(bisect_left(OB0, j), bisect_left(OB1, j))
        mys = bisect_left(myOB, j)
        delta = mys - us
        if us >= NOB:
            continue
        am[:, j, :delta * 128] = 0.0
        if mys < NOB and myOB[mys] == j:
            tri = (np.arange(128)[:, None] <= np.arange(128)[None, :])
            am[:, j, delta * 128:(delta + 1) * 128] = tri.astype(np.float32)
    import ml_dtypes
    return am.astype(ml_dtypes.bfloat16)


def kernel(L0_attn_x, L0_attn_w_q, L0_attn_w_k, L0_attn_w_v, L0_attn_w_o,
           L0_attn_gate, L0_attn_memories, L0_attn_memory_norms,
           L0_ffn_Wg, L0_ffn_W0a, L0_ffn_b0a, L0_ffn_W0b, L0_ffn_b0b,
           L0_ffn_W1a, L0_ffn_b1a, L0_ffn_W1b, L0_ffn_b1b):
    cfg = CFG()
    D, F, NB, DC, FC, T = cfg.D, cfg.F, cfg.NB, cfg.DC, cfg.FC, cfg.T
    x = np.asarray(L0_attn_x, np.float32)
    ct = lambda a: np.ascontiguousarray(np.asarray(a, np.float32))
    shared = {
        "wqT": ct(np.asarray(L0_attn_w_q).T),
        "wkT": ct(np.asarray(L0_attn_w_k).T),
        "wvT": ct(np.asarray(L0_attn_w_v).T),
        "woT": ct(np.asarray(L0_attn_w_o).T),
        "W0aT": ct(np.asarray(L0_ffn_W0a).T),
        "W0bT": ct(np.asarray(L0_ffn_W0b).T),
        "W1aT": ct(np.asarray(L0_ffn_W1a).T),
        "W1bT": ct(np.asarray(L0_ffn_W1b).T),
        "WgT": ct(np.asarray(L0_ffn_Wg).T),
        "mems": ct(L0_attn_memories),
        "normsT": ct(np.asarray(L0_attn_memory_norms).T),
        "b0a2": ct(np.asarray(L0_ffn_b0a).reshape(FC, 128).T),
        "b1a2": ct(np.asarray(L0_ffn_b1a).reshape(FC, 128).T),
        "b0b2": ct(np.asarray(L0_ffn_b0b).reshape(DC, 128).T),
        "b1b2": ct(np.asarray(L0_ffn_b1b).reshape(DC, 128).T),
        "gate": ct(np.asarray(L0_attn_gate).reshape(1, 1)),
    }
    amasks = [_host_amask(0, cfg), _host_amask(1, cfg)]
    in_maps = []
    rows_per_core = []
    for c in range(8):
        b, sub = c // 2, c % 2
        ob = own_blocks(sub, NB)
        rows = np.concatenate([np.arange(g * 128, (g + 1) * 128) for g in ob])
        rows_per_core.append((b, rows))
        m = dict(shared)
        m["x"] = np.ascontiguousarray(x[b][rows].T)
        m["amask"] = amasks[sub]
        in_maps.append(m)

    nc = build(cfg)
    res = run_bass_kernel_spmd(nc, in_maps, core_ids=list(range(8)))
    out = np.empty_like(x)
    for c in range(8):
        b, rows = rows_per_core[c]
        out[b][rows] = res.results[c]["out"].T
    return out


# version 09
# speedup vs baseline: 1.1186x; 1.0048x over previous
import numpy as np
from bisect import bisect_left
from contextlib import ExitStack
import concourse.bass as bass
import concourse.mybir as mybir
from concourse.bass_utils import run_bass_kernel_spmd

F32 = mybir.dt.float32
BF16 = mybir.dt.bfloat16
AF = mybir.ActivationFunctionType
ALU = mybir.AluOpType
AX = mybir.AxisListType

EXP_SHIFT = -20.0


class CFG:
    def __init__(self, D=2048, F=4096, S=2048, B=4, M=4):
        self.D, self.F, self.S, self.B, self.M = D, F, S, B, M
        self.DC = D // 128
        self.FC = F // 128
        self.NB = S // 128
        self.NOB = self.NB // 2
        self.T = self.NOB * 128
        self.TC = min(512, self.T)
        self.NTC = self.T // self.TC
        self.LC = min(256, self.T)
        self.NLC = self.T // self.LC


def own_blocks(sub, NB):
    return [g for g in range(NB) if ((g % 4) in (0, 3)) == (sub == 0)]


class Ctr:
    def __init__(self, es, nc, name):
        self.h = es.enter_context(nc.semaphore(name))
        self.v = 0
        self.name = name


def build(cfg):
    nc = bass.Bass("TRN2", target_bir_lowering=False, debug=False)
    D, F, T, DC, FC, M = cfg.D, cfg.F, cfg.T, cfg.DC, cfg.FC, cfg.M
    NB, NOB, TC, NTC = cfg.NB, cfg.NOB, cfg.TC, cfg.NTC
    LC, NLC = cfg.LC, cfg.NLC
    OB0, OB1 = own_blocks(0, NB), own_blocks(1, NB)
    ustart_blk = [min(bisect_left(OB0, j), bisect_left(OB1, j)) for j in range(NB)]
    owner_of = {}
    pos_of = {}
    for s_, ob in ((0, OB0), (1, OB1)):
        for p_, g in enumerate(ob):
            owner_of[g] = s_
            pos_of[g] = p_

    dr = lambda n, s, d, k="ExternalInput": nc.dram_tensor(n, s, d, kind=k)
    x_h = dr("x", [D, T], F32)
    out_h = dr("out", [D, T], F32, "ExternalOutput")
    wqT = dr("wqT", [D, D], F32)
    wkT = dr("wkT", [D, D], F32)
    wvT = dr("wvT", [D, D], F32)
    woT = dr("woT", [D, D], F32)
    W0aT_h = dr("W0aT", [D, F], F32)
    W0bT_h = dr("W0bT", [F, D], F32)
    W1aT_h = dr("W1aT", [D, F], F32)
    W1bT_h = dr("W1bT", [F, D], F32)
    WgT_h = dr("WgT", [D, 2], F32)
    mems_h = dr("mems", [M, D, D], F32)
    normsT_h = dr("normsT", [D, M], F32)
    b0a_h = dr("b0a2", [128, FC], F32)
    b1a_h = dr("b1a2", [128, FC], F32)
    b0b_h = dr("b0b2", [128, DC], F32)
    b1b_h = dr("b1b2", [128, DC], F32)
    gate_h = dr("gate", [1, 1], F32)
    amask_h = dr("amask", [128, NB, 256], BF16)
    xsp = dr("xsp", [D, T], F32, "Internal")
    kin = dr("kin", [D, T], BF16, "Internal")
    vin = dr("vin", [T, D], BF16, "Internal")
    kg = dr("kg", [2 * D, T], BF16, "Internal")
    vg = dr("vg", [2 * T, D], BF16, "Internal")
    smin = dr("smin", [1, 8], F32, "Internal")
    smg = dr("smg", [2, 8], F32, "Internal")

    r3 = lambda h: h.ap().rearrange("(c p) n -> p c n", p=128)
    x_r = r3(x_h); out_r = r3(out_h); xsp_r = r3(xsp)
    kin_r = r3(kin); vin_r = r3(vin)
    wq_r = r3(wqT); wk_r = r3(wkT); wv_r = r3(wvT); wo_r = r3(woT)
    W0a_r = r3(W0aT_h); W1a_r = r3(W1aT_h)
    W0b_r = r3(W0bT_h); W1b_r = r3(W1bT_h)
    wg_r = r3(WgT_h); norms_r = r3(normsT_h)
    mems_r = mems_h.ap().rearrange("m (c p) e -> m p c e", p=128)
    kg_r = r3(kg)
    vg_r = r3(vg)
    amask_r = amask_h.ap()

    waited = {}
    guards = {}
    floor = []

    def W(eng, engname, ctr, val):
        if val <= 0:
            return
        k = (engname, ctr.name)
        if waited.get(k, 0) >= val:
            return
        eng.wait_ge(ctr.h, val)
        waited[k] = val

    def gwait(eng, engname, key):
        g = guards.get(key)
        if g is not None:
            W(eng, engname, g[0], g[1])

    def gset(key, ctr, val):
        guards[key] = (ctr, val)

    ACOLS = max(DC * 256, FC * 128, 16)

    es0 = ExitStack()
    with es0:
        sb = lambda n, s, d: es0.enter_context(nc.sbuf_tensor(n, s, d))
        ones1 = sb("ones1", [1, 128], F32)
        ones1b = sb("ones1b", [1, 128], BF16)
        onescol = sb("onescol", [128, 1], BF16)
        b0a_t = sb("b0a_t", [128, FC], F32)
        b1a_t = sb("b1a_t", [128, FC], F32)
        b0b_t = sb("b0b_t", [128, DC], F32)
        b1b_t = sb("b1b_t", [128, DC], F32)
        wg_t = sb("wg_t", [128, DC, 2], BF16)
        norms_t = sb("norms_t", [128, DC, M], BF16)
        gate_t = sb("gate_t", [1, 1], F32)
        gsig = sb("gsig", [1, 1], F32)
        omg = sb("omg", [1, 1], F32)
        srow = sb("srow", [1, 16], F32)
        srow2 = sb("srow2", [1, 16], F32)
        wa = [sb("wa0", [128, ACOLS], BF16), sb("wa1", [128, ACOLS], BF16)]
        scrf = sb("scrf", [128, 1024], F32)
        smin_t = sb("smin_t", [1, 8], F32)
        eps_t = sb("eps_t", [1, 1], F32)
        shift_t = sb("shift_t", [128, 1], F32)
        smg_t0 = sb("smg_t0", [1, 8], F32)
        smg_t1 = sb("smg_t1", [1, 8], F32)
        amask_t = sb("amask_t", [128, NB, 256], BF16)

        SCP = es0.enter_context(nc.psum_tensor("scp", [128, 4, 512], F32))
        PS = [SCP[:, i, :] for i in range(4)] + [
            es0.enter_context(nc.psum_tensor(f"ps{i}", [128, 512], F32))
            for i in range(4, 8)]

        hd = Ctr(es0, nc, "hd")
        pd = Ctr(es0, nc, "pd")
        pw = [Ctr(es0, nc, "pw0"), Ctr(es0, nc, "pw1")]
        hsp = Ctr(es0, nc, "hsp")
        hk = Ctr(es0, nc, "hk")
        hv = Ctr(es0, nc, "hv")
        hsn = Ctr(es0, nc, "hsn")
        hsm2 = Ctr(es0, nc, "hsm2")
        hsl = Ctr(es0, nc, "hsl")
        hsl2 = Ctr(es0, nc, "hsl2")
        hx2 = [Ctr(es0, nc, "hx2a"), Ctr(es0, nc, "hx2b")]
        hout = Ctr(es0, nc, "hout")
        pe = Ctr(es0, nc, "pe")
        ac = Ctr(es0, nc, "ac")
        dv = Ctr(es0, nc, "dv")
        pl = Ctr(es0, nc, "pl")
        cc1 = Ctr(es0, nc, "cc1")
        cc2 = Ctr(es0, nc, "cc2")

        all_ctrs = [hd, pd, pe, ac, dv, pl, hsp, hk, hv, hsn, hsm2, hsl, hsl2, hout] + pw + hx2
        block = es0.enter_context(nc.Block())
        disp = {'pe': block.tensor, 'ac': block.scalar, 'dv': block.vector,
                'pl': block.gpsimd, 'sp': block.sync}

        def on(engname, f):
            def g(eng):
                for (c, v) in floor:
                    W(eng, engname, c, v)
                f(eng)
            disp[engname](g)

        def close_scope(es):
            es.close()
            for c in all_ctrs:
                floor.append((c, c.v))

        esU1 = ExitStack()
        u_t = esU1.enter_context(nc.sbuf_tensor("ua", [128, DC, T], BF16))
        esX = ExitStack()
        X = esX.enter_context(nc.sbuf_tensor("X", [128, DC, T], F32))

        def p0(sp):
            q4 = T // 4
            for qi in range(4):
                sp.dma_start(out=X[:, :, qi * q4:(qi + 1) * q4],
                             in_=x_r[:, :, qi * q4:(qi + 1) * q4]).then_inc(hd.h, 16)
                hd.v += 16
            for t, src in ((b0a_t[:], b0a_h.ap()), (b1a_t[:], b1a_h.ap()),
                           (b0b_t[:], b0b_h.ap()), (b1b_t[:], b1b_h.ap()),
                           (gate_t[:], gate_h.ap()), (amask_t[:], amask_r)):
                sp.dma_start(out=t, in_=src).then_inc(hd.h, 16)
                hd.v += 16
        on('sp', p0)
        consts_hd = hd.v
        x_ready = consts_hd

        def p0b(gp):
            gp.dma_start(out=wg_t[:], in_=wg_r).then_inc(pd.h, 16)
            gp.dma_start(out=norms_t[:], in_=norms_r).then_inc(pd.h, 16)
            pd.v += 32
            gp.memset(ones1[:], 1.0).then_inc(pl.h, 1)
            gp.memset(ones1b[:], 1.0).then_inc(pl.h, 1)
            pl.v += 1
            gp.memset(onescol[:], 1.0).then_inc(pl.h, 1)
            gp.memset(eps_t[:], 1e-5).then_inc(pl.h, 1)
            gp.memset(shift_t[:], EXP_SHIFT).then_inc(pl.h, 1)
            pl.v += 4
        on('pl', p0b)
        consts_pd = pd.v
        consts_pl = pl.v

        ln_i = [0]

        def emit_ln(Xsrc, masks, x_req, u_dst):
            x_req_t = x_req if callable(x_req) else (lambda t: x_req)
            ln_i[0] += 1
            esL = ExitStack()
            scr_b = esL.enter_context(nc.sbuf_tensor(f"lnb{ln_i[0]}", [128, 2 * DC * LC], BF16))
            rows = esL.enter_context(nc.sbuf_tensor(f"lnr{ln_i[0]}", [1, 8 * LC], F32))
            inv_d = 1.0 / D
            xqoff = DC * LC
            for t in range(NLC):
                cs, ce = t * LC, (t + 1) * LC

                def cast_body(sc, cs=cs, ce=ce, t=t):
                    W(sc, 'ac', *x_req_t(t))
                    gwait(sc, 'ac', 'lnscr')
                    for c in range(DC):
                        sc.activation(out=scr_b[:, c * LC:(c + 1) * LC],
                                      in_=Xsrc[:, c, cs:ce], func=AF.Copy)
                        ins = sc.activation(out=scr_b[:, xqoff + c * LC: xqoff + (c + 1) * LC],
                                            in_=Xsrc[:, c, cs:ce], func=AF.Square)
                    ins.then_inc(ac.h, 1); ac.v += 1
                on('ac', cast_body)
                cast_v = ac.v

                def stat_mm(te):
                    W(te, 'pe', ac, cast_v)
                    W(te, 'pe', pl, consts_pl)
                    gwait(te, 'pe', ('ps', 4))
                    gwait(te, 'pe', ('ps', 5))
                    for c in range(DC):
                        te.matmul(PS[4][0:1, :LC], onescol[:],
                                  scr_b[:, c * LC:(c + 1) * LC],
                                  start=(c == 0), stop=(c == DC - 1))
                    for c in range(DC):
                        ins = te.matmul(PS[5][0:1, :LC], onescol[:],
                                        scr_b[:, xqoff + c * LC: xqoff + (c + 1) * LC],
                                        start=(c == 0), stop=(c == DC - 1))
                    ins.then_inc(pe.h, 1); pe.v += 1
                on('pe', stat_mm)
                mm_v = pe.v
                gset('lnscr', pe, mm_v)

                mean_r = rows[0:1, 0:LC]
                var_r = rows[0:1, LC:2 * LC]
                msq_r = rows[0:1, 2 * LC:3 * LC]
                rstd_r = rows[0:1, 3 * LC:4 * LC]

                def stats(ve):
                    W(ve, 'dv', pe, mm_v)
                    gwait(ve, 'dv', 'lnrows')
                    ve.tensor_scalar_mul(mean_r, PS[4][0:1, :LC], inv_d).then_inc(dv.h, 1)
                    dv.v += 1
                    W(ve, 'dv', dv, dv.v)
                    ve.tensor_mul(msq_r, mean_r, mean_r).then_inc(dv.h, 1)
                    dv.v += 1
                    W(ve, 'dv', dv, dv.v)
                    ve.scalar_tensor_tensor(var_r, PS[5][0:1, :LC], inv_d,
                                            msq_r, ALU.mult, ALU.subtract).then_inc(dv.h, 1)
                    dv.v += 1
                on('dv', stats)
                gset(('ps', 4), dv, dv.v)
                gset(('ps', 5), dv, dv.v)
                var_v = dv.v

                def rstd1(sc):
                    W(sc, 'ac', dv, var_v)
                    W(sc, 'ac', pl, consts_pl)
                    sc.activation(out=msq_r, in_=var_r, func=AF.Sqrt,
                                  bias=eps_t[0:1, 0:1]).then_inc(ac.h, 1)
                    ac.v += 1
                on('ac', rstd1)
                sq_v = ac.v

                def rstd2(ve):
                    W(ve, 'dv', ac, sq_v)
                    ve.reciprocal(rstd_r, msq_r).then_inc(dv.h, 1)
                    dv.v += 1
                on('dv', rstd2)
                rstd_v = dv.v

                def bcast(te):
                    W(te, 'pe', dv, rstd_v)
                    gwait(te, 'pe', ('ps', 6))
                    gwait(te, 'pe', ('ps', 7))
                    te.matmul(PS[6][:, :LC], ones1[:], mean_r,
                              start=True, stop=True).then_inc(pe.h, 1)
                    te.matmul(PS[7][:, :LC], ones1[:], rstd_r,
                              start=True, stop=True).then_inc(pe.h, 1)
                    pe.v += 2
                on('pe', bcast)
                bc_v = pe.v

                def ubuild(ve, cs=cs, ce=ce):
                    W(ve, 'dv', pe, bc_v)
                    gwait(ve, 'dv', 'uarena')
                    for c in range(DC):
                        ve.tensor_sub(scrf[:, 0:LC], Xsrc[:, c, cs:ce],
                                      PS[6][:, :LC]).then_inc(dv.h, 1)
                        dv.v += 1
                        W(ve, 'dv', dv, dv.v)
                        ve.tensor_mul(u_dst[:, c, cs:ce], scrf[:, 0:LC],
                                      PS[7][:, :LC]).then_inc(dv.h, 1)
                        dv.v += 1
                        W(ve, 'dv', dv, dv.v)
                on('dv', ubuild)
                gset(('ps', 6), dv, dv.v)
                gset(('ps', 7), dv, dv.v)
                gset('lnrows', dv, dv.v)
                u_v = dv.v

                if masks is not None:
                    m0bf, m1bf, m0f, m1f = masks

                    def logit_mm(te, cs=cs, ce=ce):
                        W(te, 'pe', dv, u_v)
                        W(te, 'pe', pd, consts_pd)
                        gwait(te, 'pe', ('ps', 4))
                        gwait(te, 'pe', ('ps', 5))
                        for c in range(DC):
                            te.matmul(PS[4][0:1, :LC], wg_t[:, c, 0:1],
                                      u_dst[:, c, cs:ce], start=(c == 0), stop=(c == DC - 1))
                        for c in range(DC):
                            ins = te.matmul(PS[5][0:1, :LC], wg_t[:, c, 1:2],
                                            u_dst[:, c, cs:ce], start=(c == 0), stop=(c == DC - 1))
                        ins.then_inc(pe.h, 1); pe.v += 1
                    on('pe', logit_mm)
                    lg_v = pe.v
                    l1_r = rows[0:1, 4 * LC:5 * LC]
                    d_r = rows[0:1, 5 * LC:6 * LC]
                    mm0_r = rows[0:1, 6 * LC:7 * LC]
                    mm1_r = rows[0:1, 7 * LC:8 * LC]

                    def maskrows(ve):
                        W(ve, 'dv', pe, lg_v)
                        gwait(ve, 'dv', 'lnmrows')
                        ve.tensor_copy(l1_r, PS[5][0:1, :LC]).then_inc(dv.h, 1)
                        dv.v += 1
                        W(ve, 'dv', dv, dv.v)
                        ve.tensor_sub(d_r, PS[4][0:1, :LC], l1_r).then_inc(dv.h, 1)
                        dv.v += 1
                        W(ve, 'dv', dv, dv.v)
                        ve.tensor_single_scalar(mm0_r, d_r, 0.0, ALU.is_ge).then_inc(dv.h, 1)
                        dv.v += 1
                        W(ve, 'dv', dv, dv.v)
                        ve.tensor_scalar(mm1_r, mm0_r, -0.5, 0.5,
                                         ALU.mult, ALU.add).then_inc(dv.h, 1)
                        dv.v += 1
                        W(ve, 'dv', dv, dv.v)
                        ve.tensor_scalar_mul(mm0_r, mm0_r, 0.5).then_inc(dv.h, 1)
                        dv.v += 1
                    on('dv', maskrows)
                    gset(('ps', 4), dv, dv.v)
                    gset(('ps', 5), dv, dv.v)
                    mr_v = dv.v

                    def mask_bcast(te):
                        W(te, 'pe', dv, mr_v)
                        gwait(te, 'pe', ('ps', 6))
                        gwait(te, 'pe', ('ps', 7))
                        te.matmul(PS[6][:, :LC], ones1[:], mm0_r,
                                  start=True, stop=True).then_inc(pe.h, 1)
                        te.matmul(PS[7][:, :LC], ones1[:], mm1_r,
                                  start=True, stop=True).then_inc(pe.h, 1)
                        pe.v += 2
                    on('pe', mask_bcast)
                    mb_v = pe.v
                    gset('lnmrows', pe, mb_v)

                    def mask_copy(ve, cs=cs, ce=ce):
                        W(ve, 'dv', pe, mb_v)
                        ve.tensor_copy(m0f[:, cs:ce], PS[6][:, :LC]).then_inc(dv.h, 1)
                        ve.tensor_copy(m0bf[:, cs:ce], PS[6][:, :LC]).then_inc(dv.h, 1)
                        ve.tensor_copy(m1f[:, cs:ce], PS[7][:, :LC]).then_inc(dv.h, 1)
                        ve.tensor_copy(m1bf[:, cs:ce], PS[7][:, :LC]).then_inc(dv.h, 1)
                        dv.v += 4
                    on('dv', mask_copy)
                    gset(('ps', 6), dv, dv.v)
                    gset(('ps', 7), dv, dv.v)
            gset('lnscr', dv, dv.v)
            u_done = dv.v
            close_scope(esL)
            return u_done

        def emit_ffn(Xdst, u_ready, masks, tag, u_src):
            m0bf, m1bf, m0f, m1f = masks
            esF = ExitStack()
            h_sb = esF.enter_context(nc.sbuf_tensor(f"h{tag}", [128, FC * T], BF16))
            scr2 = esF.enter_context(nc.sbuf_tensor(f"s2{tag}", [128, 512], F32))
            SUPA = 256
            NSA = F // SUPA
            xadd = {}
            for ei, (Wa_r, Wb_r, ba, bb, fn, mbf, mf) in enumerate((
                    (W0a_r, W0b_r, b0a_t, b0b_t, AF.Gelu, m0bf, m0f),
                    (W1a_r, W1b_r, b1a_t, b1b_t, AF.Relu, m1bf, m1f))):
                loadv = {}

                def emit_load_a(sj, Wa_r=Wa_r, loadv=loadv):
                    def body(gp, sj=sj):
                        gwait(gp, 'pl', ('arena', sj % 2))
                        w_ = pw[sj % 2]
                        hc = DC // 2
                        gp.dma_start(out=wa[sj % 2][:, :hc * SUPA],
                                     in_=Wa_r[:, :hc, sj * SUPA:(sj + 1) * SUPA]
                                     ).then_inc(w_.h, 16)
                        gp.dma_start(out=wa[sj % 2][:, hc * SUPA:DC * SUPA],
                                     in_=Wa_r[:, hc:, sj * SUPA:(sj + 1) * SUPA]
                                     ).then_inc(w_.h, 16)
                        w_.v += 32
                        loadv[sj] = (w_, w_.v)
                    on('pl', body)
                emit_load_a(0)
                if NSA > 1:
                    emit_load_a(1)
                for sj in range(NSA):
                    last_mm = 0
                    for fo in range(SUPA // 128):
                        f = sj * (SUPA // 128) + fo
                        for tc in range(NTC):
                            cs, ce = tc * TC, (tc + 1) * TC
                            slot = (f * NTC + tc) % 2

                            def mm_body(te, sj=sj, fo=fo, cs=cs, ce=ce, slot=slot):
                                W(te, 'pe', *loadv[sj])
                                W(te, 'pe', dv, u_ready)
                                gwait(te, 'pe', ('ps', slot))
                                for c in range(DC):
                                    ins = te.matmul(
                                        PS[slot][:, :TC],
                                        wa[sj % 2][:, c * SUPA + fo * 128: c * SUPA + (fo + 1) * 128],
                                        u_src[:, c, cs:ce],
                                        start=(c == 0), stop=(c == DC - 1))
                                ins.then_inc(pe.h, 1); pe.v += 1
                            on('pe', mm_body)
                            mm_v = pe.v
                            last_mm = mm_v

                            def act_body(sc, f=f, cs=cs, ce=ce, slot=slot, mm_v=mm_v, fn=fn, ba=ba):
                                W(sc, 'ac', pe, mm_v)
                                W(sc, 'ac', hd, consts_hd)
                                gwait(sc, 'ac', ('hsb',))
                                sc.activation(out=h_sb[:, f * T + cs: f * T + ce],
                                              in_=PS[slot][:, :TC], func=fn,
                                              bias=ba[:, f:f + 1]).then_inc(ac.h, 1)
                                ac.v += 1
                            on('ac', act_body)
                            gset(('ps', slot), ac, ac.v)
                            act_v = ac.v

                            def mask_body(ve, f=f, cs=cs, ce=ce, act_v=act_v, mbf=mbf):
                                W(ve, 'dv', ac, act_v)
                                ve.tensor_mul(h_sb[:, f * T + cs: f * T + ce],
                                              h_sb[:, f * T + cs: f * T + ce],
                                              mbf[:, cs:ce]).then_inc(dv.h, 1)
                                dv.v += 1
                            on('dv', mask_body)
                    gset(('arena', sj % 2), pe, last_mm)
                    if sj + 2 < NSA:
                        emit_load_a(sj + 2)
                h_ready = dv.v
                SUPB = 128
                NSB = D // SUPB
                loadvb = {}

                def emit_load_b(es_, Wb_r=Wb_r, loadvb=loadvb):
                    def body(gp, es_=es_):
                        gwait(gp, 'pl', ('arena', es_ % 2))
                        w_ = pw[es_ % 2]
                        hc = FC // 2
                        gp.dma_start(out=wa[es_ % 2][:, :hc * SUPB],
                                     in_=Wb_r[:, :hc, es_ * SUPB:(es_ + 1) * SUPB]
                                     ).then_inc(w_.h, 16)
                        gp.dma_start(out=wa[es_ % 2][:, hc * SUPB:FC * SUPB],
                                     in_=Wb_r[:, hc:, es_ * SUPB:(es_ + 1) * SUPB]
                                     ).then_inc(w_.h, 16)
                        w_.v += 32
                        loadvb[es_] = (w_, w_.v)
                    on('pl', body)
                emit_load_b(0)
                if NSB > 1:
                    emit_load_b(1)
                for es_ in range(NSB):
                    e = es_
                    last_mm = 0
                    for tc in range(NTC):
                        cs, ce = tc * TC, (tc + 1) * TC
                        slot = 2 + (e * NTC + tc) % 2

                        def mmb(te, es_=es_, cs=cs, ce=ce, slot=slot, h_ready=h_ready):
                            W(te, 'pe', *loadvb[es_])
                            W(te, 'pe', dv, h_ready)
                            gwait(te, 'pe', ('ps', slot))
                            for j in range(FC):
                                ins = te.matmul(PS[slot][:, :TC],
                                                wa[es_ % 2][:, j * SUPB:(j + 1) * SUPB],
                                                h_sb[:, j * T + cs: j * T + ce],
                                                start=(j == 0), stop=(j == FC - 1))
                            ins.then_inc(pe.h, 1); pe.v += 1
                        on('pe', mmb)
                        mm_v = pe.v
                        last_mm = mm_v

                        def evac(ve, e=e, cs=cs, ce=ce, slot=slot, mm_v=mm_v, mf=mf, bb=bb):
                            W(ve, 'dv', pe, mm_v)
                            W(ve, 'dv', hd, consts_hd)
                            gwait(ve, 'dv', 'scr2')
                            ve.scalar_tensor_tensor(scr2[:, :TC], mf[:, cs:ce],
                                                    bb[:, e:e + 1], PS[slot][:, :TC],
                                                    ALU.mult, ALU.add).then_inc(dv.h, 1)
                            dv.v += 1
                            W(ve, 'dv', dv, dv.v)
                            if (e, cs) in xadd:
                                W(ve, 'dv', dv, xadd[(e, cs)])
                            ve.tensor_add(Xdst[:, e, cs:ce], Xdst[:, e, cs:ce],
                                          scr2[:, :TC]).then_inc(dv.h, 1)
                            dv.v += 1
                            xadd[(e, cs)] = dv.v
                            gset('scr2', dv, dv.v)
                        on('dv', evac)
                        gset(('ps', slot), dv, dv.v)
                    gset(('arena', es_ % 2), pe, last_mm)
                    if es_ + 2 < NSB:
                        emit_load_b(es_ + 2)
                gset(('hsb',), pe, last_mm)
            gset('uarena', pe, pe.v)
            ffn_done = dv.v
            close_scope(esF)
            return ffn_done, xadd

        esM1 = ExitStack()
        masks1 = tuple(esM1.enter_context(nc.sbuf_tensor(n, [128, T], d))
                       for n, d in (("m0bf", BF16), ("m1bf", BF16),
                                    ("m0f", F32), ("m1f", F32)))
        nq = max(1, NLC // 4)
        u1_v = emit_ln(X, masks1,
                       lambda t: (hd, 16 * min(4, t // nq + 1)), u_t)
        emit_ffn(X, u1_v, masks1, "f1", u_t)
        close_scope(esM1)

        u2_v = emit_ln(X, None, (dv, dv.v), u_t)

        def spill(sp):
            W(sp, 'sp', dv, u2_v)
            sp.dma_start(out=xsp_r, in_=X[:, :, :]).then_inc(hsp.h, 16)
            hsp.v += 16
        on('sp', spill)
        close_scope(esX)

        esQ = ExitStack()
        qs = esQ.enter_context(nc.sbuf_tensor("qs", [128, DC, T], BF16, side="right"))
        esB_ = ExitStack()
        sq = esB_.enter_context(nc.sbuf_tensor("sq", [128, DC, T], BF16, side="right"))
        raws = esB_.enter_context(nc.sbuf_tensor("raws", [1, M * T], BF16, side="right"))
        scr2b = esB_.enter_context(nc.sbuf_tensor("scr2b", [128, 512], BF16, side="right"))
        esA_ = ExitStack()
        kvst = esA_.enter_context(nc.sbuf_tensor("kvst", [128, DC * T], BF16, side="right"))
        inv_sqrt_hd = float(1.0 / np.sqrt(128.0))

        SUPQ = 256
        NSQ = D // SUPQ

        def emit_proj(Wh_r, kind):
            loadv = {}

            def emit_load(os_):
                def body(gp, os_=os_):
                    gwait(gp, 'pl', ('arena', os_ % 2))
                    w_ = pw[os_ % 2]
                    hc = DC // 2
                    gp.dma_start(out=wa[os_ % 2][:, :hc * SUPQ],
                                 in_=Wh_r[:, :hc, os_ * SUPQ:(os_ + 1) * SUPQ]
                                 ).then_inc(w_.h, 16)
                    gp.dma_start(out=wa[os_ % 2][:, hc * SUPQ:DC * SUPQ],
                                 in_=Wh_r[:, hc:, os_ * SUPQ:(os_ + 1) * SUPQ]
                                 ).then_inc(w_.h, 16)
                    w_.v += 32
                    loadv[os_] = (w_, w_.v)
                on('pl', body)
            emit_load(0)
            if NSQ > 1:
                emit_load(1)
            for os_ in range(NSQ):
                last_mm = 0
                if kind != 'v':
                    for ko in range(SUPQ // 128):
                        k = os_ * (SUPQ // 128) + ko
                        for tc in range(NTC):
                            cs, ce = tc * TC, (tc + 1) * TC
                            slot = (k * NTC + tc) % 2

                            def mmq(te, os_=os_, ko=ko, cs=cs, ce=ce, slot=slot):
                                W(te, 'pe', *loadv[os_])
                                W(te, 'pe', dv, u2_v)
                                gwait(te, 'pe', ('ps', slot))
                                for c in range(DC):
                                    ins = te.matmul(
                                        PS[slot][:, :TC],
                                        wa[os_ % 2][:, c * SUPQ + ko * 128: c * SUPQ + (ko + 1) * 128],
                                        u_t[:, c, cs:ce],
                                        start=(c == 0), stop=(c == DC - 1))
                                ins.then_inc(pe.h, 1); pe.v += 1
                            on('pe', mmq)
                            mm_v = pe.v
                            last_mm = mm_v
                            if kind == 'q':
                                def evq(sc, k=k, cs=cs, ce=ce, slot=slot, mm_v=mm_v):
                                    W(sc, 'ac', pe, mm_v)
                                    gwait(sc, 'ac', 'scr2b')
                                    gwait(sc, 'ac', 'scrfq')
                                    sc.activation(out=qs[:, k, cs:ce], in_=PS[slot][:, :TC],
                                                  func=AF.Copy, scale=inv_sqrt_hd)
                                    sc.activation(out=sq[:, k, cs:ce], in_=PS[slot][:, :TC],
                                                  func=AF.Relu)
                                    sc.activation(out=scrf[:, 512:512 + TC],
                                                  in_=PS[slot][:, :TC],
                                                  func=AF.Relu, scale=-1.0).then_inc(ac.h, 1)
                                    ac.v += 1
                                    W(sc, 'ac', ac, ac.v)
                                    sc.activation(out=scr2b[:, :TC],
                                                  in_=scrf[:, 512:512 + TC],
                                                  func=AF.Exp, scale=-1.0).then_inc(ac.h, 1)
                                    ac.v += 1
                                on('ac', evq)
                                gset(('ps', slot), ac, ac.v)
                                gset('scrfq', ac, ac.v)
                                e_v = ac.v

                                def addsig(ve, k=k, cs=cs, ce=ce, e_v=e_v):
                                    W(ve, 'dv', ac, e_v)
                                    ve.tensor_add(sq[:, k, cs:ce], sq[:, k, cs:ce],
                                                  scr2b[:, :TC]).then_inc(dv.h, 1)
                                    dv.v += 1
                                    gset('scr2b', dv, dv.v)
                                on('dv', addsig)
                            else:
                                def evk(sc, k=k, cs=cs, ce=ce, slot=slot, mm_v=mm_v):
                                    W(sc, 'ac', pe, mm_v)
                                    gwait(sc, 'ac', ('kvst',))
                                    sc.activation(out=kvst[:, k * T + cs: k * T + ce],
                                                  in_=PS[slot][:, :TC],
                                                  func=AF.Copy).then_inc(ac.h, 1)
                                    ac.v += 1
                                on('ac', evk)
                                gset(('ps', slot), ac, ac.v)
                else:
                    for tt in range(NOB):
                        slot = tt % 2

                        def mmv(te, os_=os_, tt=tt, slot=slot):
                            W(te, 'pe', *loadv[os_])
                            W(te, 'pe', dv, u2_v)
                            gwait(te, 'pe', ('ps', slot))
                            for c in range(DC):
                                ins = te.matmul(PS[slot][:, :SUPQ],
                                                u_t[:, c, tt * 128:(tt + 1) * 128],
                                                wa[os_ % 2][:, c * SUPQ:(c + 1) * SUPQ],
                                                start=(c == 0), stop=(c == DC - 1))
                            ins.then_inc(pe.h, 1); pe.v += 1
                        on('pe', mmv)
                        mm_v = pe.v
                        last_mm = mm_v

                        def evv(sc, os_=os_, tt=tt, slot=slot, mm_v=mm_v):
                            W(sc, 'ac', pe, mm_v)
                            gwait(sc, 'ac', ('kvst',))
                            sc.activation(out=kvst[:, tt * D + os_ * SUPQ: tt * D + (os_ + 1) * SUPQ],
                                          in_=PS[slot][:, :SUPQ],
                                          func=AF.Copy).then_inc(ac.h, 1)
                            ac.v += 1
                        on('ac', evv)
                        gset(('ps', slot), ac, ac.v)
                gset(('arena', os_ % 2), pe, last_mm)
                if os_ + 2 < NSQ:
                    emit_load(os_ + 2)

        emit_proj(wq_r, 'q')
        sig_done = dv.v
        q_done = ac.v
        emit_proj(wk_r, 'k')
        k_done = ac.v

        def send_k(sp):
            W(sp, 'sp', ac, k_done)
            sp.dma_start(out=kin_r,
                         in_=kvst[:].rearrange("p (c t) -> p c t", c=DC)
                         ).then_inc(hk.h, 16)
            hk.v += 16
        on('sp', send_k)
        gset(('kvst',), hk, hk.v)
        kin_v = hk.v

        emit_proj(wv_r, 'v')
        v_done = ac.v
        gset('uarena', pe, pe.v)

        def send_v(sp):
            W(sp, 'sp', ac, v_done)
            sp.dma_start(out=vin_r,
                         in_=kvst[:].rearrange("p (b d) -> p b d", b=NOB)
                         ).then_inc(hv.h, 16)
            hv.v += 16
        on('sp', send_v)
        vin_v = hv.v

        for m in range(M):
            for tc in range(NTC):
                cs, ce = tc * TC, (tc + 1) * TC
                slot = 4 + tc % 2

                def mmr(te, m=m, cs=cs, ce=ce, slot=slot):
                    W(te, 'pe', dv, sig_done)
                    W(te, 'pe', pd, consts_pd)
                    gwait(te, 'pe', ('ps', slot))
                    for c in range(DC):
                        ins = te.matmul(PS[slot][0:1, :TC], norms_t[:, c, m:m + 1],
                                        sq[:, c, cs:ce], start=(c == 0), stop=(c == DC - 1))
                    ins.then_inc(pe.h, 1); pe.v += 1
                on('pe', mmr)
                mm_v = pe.v

                def evr(ve, m=m, cs=cs, ce=ce, slot=slot, mm_v=mm_v, tc=tc):
                    W(ve, 'dv', pe, mm_v)
                    ve.tensor_copy(raws[0:1, m * T + cs: m * T + ce],
                                   PS[slot][0:1, :TC]).then_inc(dv.h, 1)
                    ve.reduce_sum(srow2[0:1, m * NTC + tc: m * NTC + tc + 1],
                                  PS[slot][0:1, :TC], axis=AX.X).then_inc(dv.h, 1)
                    dv.v += 2
                on('dv', evr)
                gset(('ps', slot), dv, dv.v)

        def relsum(ve):
            W(ve, 'dv', dv, dv.v)
            ve.memset(smin_t[0:1, 0:8], 0.0).then_inc(dv.h, 1)
            dv.v += 1
            W(ve, 'dv', dv, dv.v)
            if NTC == 2:
                ve.tensor_add(smin_t[0:1, 0:M],
                              srow2[0:1, 0:2 * M:2],
                              srow2[0:1, 1:2 * M:2]).then_inc(dv.h, 1)
            else:
                ve.tensor_copy(smin_t[0:1, 0:M], srow2[0:1, 0:M]).then_inc(dv.h, 1)
            dv.v += 1
        on('dv', relsum)
        rel_v = dv.v

        def send_sm(sp):
            W(sp, 'sp', dv, rel_v)
            sp.dma_start(out=smin.ap(), in_=smin_t[:]).then_inc(hsn.h, 16)
            hsn.v += 16
        on('sp', send_sm)
        sm_v = hsn.v

        def colls(gp):
            W(gp, 'pl', hsn, sm_v)
            gp.collective_compute("AllGather", ALU.bypass,
                                  replica_groups=[[0, 1], [2, 3], [4, 5], [6, 7]],
                                  ins=[smin.ap()], outs=[smg.ap()]).then_inc(cc2.h, 1)
            cc2.v += 1
            W(gp, 'pl', hk, kin_v)
            gp.collective_compute("AllGather", ALU.bypass,
                                  replica_groups=[[0, 1], [2, 3], [4, 5], [6, 7]],
                                  ins=[kin.ap()], outs=[kg.ap()]).then_inc(cc1.h, 1)
            cc1.v += 1
            W(gp, 'pl', hv, vin_v)
            gp.collective_compute("AllGather", ALU.bypass,
                                  replica_groups=[[0, 1], [2, 3], [4, 5], [6, 7]],
                                  ins=[vin.ap()], outs=[vg.ap()]).then_inc(cc1.h, 1)
            cc1.v += 1
        on('pl', colls)
        close_scope(esA_)
        close_scope(esU1)

        esP = ExitStack()
        pre = esP.enter_context(nc.sbuf_tensor("pre", [128, DC, T], F32))
        esS = ExitStack()
        kslab = [esS.enter_context(nc.sbuf_tensor(f"ks{o}", [128, 2, T], BF16, side="right"))
                 for o in range(2)]
        vslab = [esS.enter_context(nc.sbuf_tensor(f"vs{o}", [128, 2, NOB, 128], BF16, side="right"))
                 for o in range(2)]
        attnw = esS.enter_context(nc.sbuf_tensor("attnw", [128, 2, 4, 512], BF16, side="right"))
        lrow = esS.enter_context(nc.sbuf_tensor("lrow", [1, T], F32, side="right"))
        cbc = [esS.enter_context(nc.sbuf_tensor(f"cbc{i}", [128, T], BF16, side="right"))
               for i in range(2)]
        mscr = esS.enter_context(nc.sbuf_tensor("mscr", [128, 2, 512], F32, side="right"))
        ocol = esS.enter_context(nc.sbuf_tensor("ocol", [128, 1], F32, side="right"))
        wcol = esS.enter_context(nc.sbuf_tensor("wcol", [128, 4], F32, side="right"))

        def smrecv(sp):
            W(sp, 'sp', cc2, 1)
            sp.dma_start(out=smg_t0[:], in_=smg.ap()[0:1, :]).then_inc(hsm2.h, 16)
            sp.dma_start(out=smg_t1[:], in_=smg.ap()[1:2, :]).then_inc(hsm2.h, 16)
            hsm2.v += 32
        on('sp', smrecv)
        smg_v = hsm2.v

        def relcomb(ve):
            W(ve, 'dv', hsm2, smg_v)
            ve.tensor_add(srow[0:1, 0:M], smg_t0[0:1, 0:M],
                          smg_t1[0:1, 0:M]).then_inc(dv.h, 1)
            dv.v += 1
            W(ve, 'dv', dv, dv.v)
            ve.tensor_scalar_mul(srow[0:1, 0:M], srow[0:1, 0:M],
                                 1.0 / cfg.S).then_inc(dv.h, 1)
            dv.v += 1
            W(ve, 'dv', dv, dv.v)
            ve.reduce_max(srow[0:1, 8:9], srow[0:1, 0:M], axis=AX.X).then_inc(dv.h, 1)
            dv.v += 1
            W(ve, 'dv', dv, dv.v)
            ve.tensor_scalar_mul(srow[0:1, 9:10], srow[0:1, 8:9], -1.0).then_inc(dv.h, 1)
            dv.v += 1
        on('dv', relcomb)
        rr_v = dv.v

        def relexp(sc):
            W(sc, 'ac', dv, rr_v)
            W(sc, 'ac', hd, consts_hd)
            sc.activation(out=srow[0:1, 4:4 + M], in_=srow[0:1, 0:M],
                          func=AF.Exp, bias=srow[0:1, 9:10]).then_inc(ac.h, 1)
            sc.activation(out=gsig[:], in_=gate_t[:],
                          func=AF.Sigmoid).then_inc(ac.h, 1)
            ac.v += 2
        on('ac', relexp)
        re_v = ac.v

        def relw(ve):
            W(ve, 'dv', ac, re_v)
            ve.reduce_sum(srow[0:1, 10:11], srow[0:1, 4:4 + M], axis=AX.X).then_inc(dv.h, 1)
            dv.v += 1
            W(ve, 'dv', dv, dv.v)
            ve.reciprocal(srow[0:1, 11:12], srow[0:1, 10:11]).then_inc(dv.h, 1)
            dv.v += 1
            W(ve, 'dv', dv, dv.v)
            ve.tensor_scalar_mul(srow[0:1, 12:12 + M], srow[0:1, 4:4 + M],
                                 srow[0:1, 11:12]).then_inc(dv.h, 1)
            dv.v += 1
            W(ve, 'dv', dv, dv.v)
            ve.tensor_scalar_mul(srow[0:1, 12:12 + M], srow[0:1, 12:12 + M],
                                 gsig[0:1, 0:1]).then_inc(dv.h, 1)
            dv.v += 1
            W(ve, 'dv', dv, dv.v)
            ve.tensor_scalar(omg[:], gsig[:], -1.0, 1.0, ALU.mult, ALU.add).then_inc(dv.h, 1)
            dv.v += 1
        on('dv', relw)
        gw_v = dv.v

        def cbcol(te):
            W(te, 'pe', dv, gw_v)
            W(te, 'pe', pl, consts_pl)
            gwait(te, 'pe', ('ps', 6))
            te.matmul(PS[6][:, 0:1], ones1[:], omg[0:1, 0:1],
                      start=True, stop=True)
            te.matmul(PS[6][:, 1:5], ones1[:], srow[0:1, 12:12 + M],
                      start=True, stop=True).then_inc(pe.h, 1)
            pe.v += 1
        on('pe', cbcol)
        cbcol_v = pe.v

        def cbcol2(ve):
            W(ve, 'dv', pe, cbcol_v)
            ve.tensor_copy(ocol[:, :], PS[6][:, 0:1]).then_inc(dv.h, 1)
            ve.tensor_copy(wcol[:, :], PS[6][:, 1:5]).then_inc(dv.h, 1)
            dv.v += 2
        on('dv', cbcol2)
        gset(('ps', 6), dv, dv.v)

        SUPM = 256
        NSM = D // SUPM
        pre_w = {}
        mem_pairs = [(m, es_) for m in range(M) for es_ in range(NSM)]
        mem_loadv = {}

        def emit_load_m(p_i):
            m_, e_ = mem_pairs[p_i]

            def body(gp, p_i=p_i, m_=m_, e_=e_):
                gwait(gp, 'pl', ('arena', p_i % 2))
                w_ = pw[p_i % 2]
                hc = DC // 2
                gp.dma_start(out=wa[p_i % 2][:, :hc * SUPM],
                             in_=mems_r[m_, :, :hc, e_ * SUPM:(e_ + 1) * SUPM]
                             ).then_inc(w_.h, 16)
                gp.dma_start(out=wa[p_i % 2][:, hc * SUPM:DC * SUPM],
                             in_=mems_r[m_, :, hc:, e_ * SUPM:(e_ + 1) * SUPM]
                             ).then_inc(w_.h, 16)
                w_.v += 32
                mem_loadv[p_i] = (w_, w_.v)
            on('pl', body)

        mchain_i = [0]

        def memq_gen():
            emit_load_m(0)
            emit_load_m(1)
            for p_i, (m, es_) in enumerate(mem_pairs):
                if es_ == 0:
                    def crow(ve, m=m):
                        W(ve, 'dv', dv, gw_v)
                        es_lp = nc.allow_low_precision(reason="bf16 in-place clamp of raw row (reference-tolerant path)")
                        es_lp.__enter__()
                        ve.tensor_scalar_max(raws[0:1, m * T:(m + 1) * T],
                                             raws[0:1, m * T:(m + 1) * T], 1e-6).then_inc(dv.h, 1)
                        dv.v += 1
                        es_lp.__exit__(None, None, None)
                    yield ('crow', m, crow)
                    for tc2 in range(NTC):
                        def cbcast(te, m=m, tc2=tc2):
                            W(te, 'pe', dv, dv.v)
                            W(te, 'pe', pl, consts_pl)
                            gwait(te, 'pe', ('ps', 6))
                            te.matmul(PS[6][:, :TC], ones1b[:],
                                      raws[0:1, m * T + tc2 * TC: m * T + (tc2 + 1) * TC],
                                      start=True, stop=True).then_inc(pe.h, 1)
                            pe.v += 1
                        yield ('cbcast', m, tc2, cbcast)
                for eo in range(SUPM // 128):
                    e = es_ * (SUPM // 128) + eo
                    for tc2 in range(NTC):
                        cs, ce = tc2 * TC, (tc2 + 1) * TC
                        slot = 6 + mchain_i[0] % 2
                        mchain_i[0] += 1

                        def chain(te, p_i=p_i, eo=eo, cs=cs, ce=ce, slot=slot):
                            W(te, 'pe', *mem_loadv[p_i])
                            W(te, 'pe', dv, sig_done)
                            gwait(te, 'pe', ('ps', slot))
                            for c in range(DC):
                                ins = te.matmul(
                                    PS[slot][:, :TC],
                                    wa[p_i % 2][:, c * SUPM + eo * 128: c * SUPM + (eo + 1) * 128],
                                    sq[:, c, cs:ce],
                                    start=(c == 0), stop=(c == DC - 1))
                            ins.then_inc(pe.h, 1); pe.v += 1

                        def evm(ve, m=m, e=e, cs=cs, ce=ce, slot=slot):
                            half = slot - 6
                            ve.tensor_mul(mscr[:, half, :], PS[slot][:, :TC],
                                          cbc[m % 2][:, cs:ce]).then_inc(dv.h, 1)
                            dv.v += 1
                            W(ve, 'dv', dv, dv.v)
                            if (e, cs) in pre_w:
                                W(ve, 'dv', *pre_w[(e, cs)])
                            if m == 0:
                                ve.tensor_copy(pre[:, e, cs:ce], mscr[:, half, :]).then_inc(dv.h, 1)
                            else:
                                ve.tensor_add(pre[:, e, cs:ce], pre[:, e, cs:ce],
                                              mscr[:, half, :]).then_inc(dv.h, 1)
                            dv.v += 1
                            pre_w[(e, cs)] = (dv, dv.v)
                        yield ('chain', p_i, chain, evm, slot)
                if p_i + 2 < len(mem_pairs):
                    emit_load_m(p_i + 2)

        memq = memq_gen()
        mem_done = [False]

        def emit_one_memq():
            try:
                item = next(memq)
            except StopIteration:
                mem_done[0] = True
                return
            if item[0] == 'crow':
                on('dv', item[2])
            elif item[0] == 'cbcast':
                _, m, tc2, f = item
                on('pe', f)
                cb_v = pe.v

                def cbrec(ve, m=m, tc2=tc2, cb_v=cb_v):
                    W(ve, 'dv', pe, cb_v)
                    ve.reciprocal(scrf[:, 0:TC], PS[6][:, :TC]).then_inc(dv.h, 1)
                    dv.v += 1
                    W(ve, 'dv', dv, dv.v)
                    ve.tensor_scalar_mul(cbc[m % 2][:, tc2 * TC:(tc2 + 1) * TC],
                                         scrf[:, 0:TC],
                                         wcol[:, m:m + 1]).then_inc(dv.h, 1)
                    dv.v += 1
                on('dv', cbrec)
                gset(('ps', 6), dv, dv.v)
            else:
                _, p_i, chain, evm, slot = item
                on('pe', chain)
                mm_v = pe.v

                def evm_w(ve, evm=evm, mm_v=mm_v):
                    W(ve, 'dv', pe, mm_v)
                    evm(ve)
                on('dv', evm_w)
                gset(('ps', slot), dv, dv.v - 1)
                gset(('arena', p_i % 2), pe, mm_v)

        N_MEMQ = M * (3 + NSM * (SUPM // 128) * NTC)
        FRONT = 40
        for _ in range(FRONT):
            emit_one_memq()
        assert len(pre_w) >= DC * NTC, "front-load must cover memory 0 fully"

        def parts_of(tc):
            out = []
            cs0, ce0 = tc * TC, (tc + 1) * TC
            for j in range(NB):
                a = max(cs0, ustart_blk[j] * 128)
                if a < ce0:
                    out.append((j, a, ce0))
            return out

        packed = {}
        for tc in range(NTC):
            parts = parts_of(tc)
            slices = []
            cur, cur_w = [], 0
            for (j, a, b) in parts:
                w = b - a
                if cur_w + w > 512:
                    slices.append(cur); cur, cur_w = [], 0
                cur.append((j, a, b, cur_w)); cur_w += w
            if cur:
                slices.append(cur)
            groups = [slices[i:i + 4] for i in range(0, len(slices), 4)]
            packed[tc] = (parts, groups)

        aw_guard = {}
        memq_budget = [0.0]
        n_groups_total = DC * sum(len(packed[tc][1]) for tc in range(NTC))
        memq_per_group = float(N_MEMQ - FRONT) / max(1, n_groups_total)

        slab_v = 0
        slab_hist = {}
        for h in range(DC):
            par = h % 2

            hsl_p = hsl if par == 0 else hsl2

            def slabload(sp, h=h, par=par, hsl_p=hsl_p):
                W(sp, 'sp', cc1, 2)
                gwait(sp, 'sp', ('slab', par))
                for o in range(2):
                    sp.dma_start(out=kslab[o][:, par, :],
                                 in_=kg_r[:, o * DC + h, :]).then_inc(hsl_p.h, 16)
                    sp.dma_start(out=vslab[o][:, par, :, :],
                                 in_=vg_r[:, o * NOB:(o + 1) * NOB,
                                          h * 128:(h + 1) * 128]).then_inc(hsl_p.h, 16)
                    hsl_p.v += 32
            on('sp', slabload)
            slab_v = hsl_p.v
            slab_last = 0
            for tc in range(NTC):
                parts, groups = packed[tc]
                first_j = parts[0][0]
                last_j = parts[-1][0]
                for gi, grp in enumerate(groups):
                    n_sl = len(grp)
                    def scores(te, h=h, par=par, grp=grp, slab_v=slab_v, hsl_p=hsl_p):
                        W(te, 'pe', hsl_p, slab_v)
                        W(te, 'pe', ac, q_done)
                        gwait(te, 'pe', ('sc',))
                        for bi in range(4):
                            gwait(te, 'pe', ('ps', bi))
                        for si, sl in enumerate(grp):
                            for (j, a, b, off) in sl:
                                o = owner_of[j]
                                pj = pos_of[j]
                                ins = te.matmul(SCP[:, si, off:off + b - a],
                                                kslab[o][:, par, pj * 128:(pj + 1) * 128],
                                                qs[:, h, a:b],
                                                start=True, stop=True)
                        ins.then_inc(pe.h, 1); pe.v += 1
                    on('pe', scores)
                    sc_v = pe.v
                    slab_last = sc_v
                    memq_budget[0] += memq_per_group
                    while memq_budget[0] >= 1.0 and not mem_done[0]:
                        emit_one_memq()
                        memq_budget[0] -= 1.0

                    half = (h * 16 + tc * 8 + gi) % 2

                    def expg(sc_, sc_v=sc_v, half=half, n_sl=n_sl):
                        W(sc_, 'ac', pe, sc_v)
                        W(sc_, 'ac', pl, consts_pl)
                        gwait(sc_, 'ac', ('aw', half))
                        sc_.activation(
                            out=attnw[:, half, 0:n_sl, :].rearrange("p a b -> p (a b)"),
                            in_=SCP[:, 0:n_sl, :].rearrange("p a b -> p (a b)"),
                            func=AF.Exp, bias=shift_t[:, 0:1]).then_inc(ac.h, 1)
                        ac.v += 1
                    on('ac', expg)
                    exp_v = ac.v
                    gset(('sc',), ac, exp_v)

                    mask_ops = []
                    for si, sl in enumerate(grp):
                        for (j, a, b, off) in sl:
                            ms = ustart_blk[j] * 128
                            me = min(ms + 256, T)
                            x0, x1 = max(a, ms), min(b, me)
                            if x0 < x1:
                                mask_ops.append((si, j, a, off, x0, x1, ms))
                    mask_v = 0
                    if mask_ops:
                        def maskg(ve, mask_ops=mask_ops, half=half, exp_v=exp_v):
                            W(ve, 'dv', ac, exp_v)
                            W(ve, 'dv', hd, consts_hd)
                            for (si, j, a, off, x0, x1, ms) in mask_ops:
                                ins = ve.tensor_mul(
                                    attnw[:, half, si, off + x0 - a: off + x1 - a],
                                    attnw[:, half, si, off + x0 - a: off + x1 - a],
                                    amask_t[:, j, x0 - ms: x1 - ms])
                            ins.then_inc(dv.h, 1); dv.v += 1
                        on('dv', maskg)
                        mask_v = dv.v

                    def lav(te, h=h, tc=tc, par=par, grp=grp, half=half, exp_v=exp_v,
                            mask_v=mask_v, first_j=first_j, last_j=last_j):
                        W(te, 'pe', ac, exp_v)
                        if mask_v:
                            W(te, 'pe', dv, mask_v)
                        gwait(te, 'pe', ('ps', 4))
                        gwait(te, 'pe', ('ps', 5))
                        co = tc * TC
                        for si, sl in enumerate(grp):
                            for (j, a, b, off) in sl:
                                o = owner_of[j]
                                pj = pos_of[j]
                                st = (j == first_j)
                                sp_ = (j == last_j)
                                te.matmul(PS[5][0:1, a - co: b - co],
                                          onescol[:],
                                          attnw[:, half, si, off:off + b - a],
                                          start=st, stop=sp_)
                                ins = te.matmul(PS[4][:, a - co: b - co],
                                                vslab[o][:, par, pj, :],
                                                attnw[:, half, si, off:off + b - a],
                                                start=st, stop=sp_)
                        ins.then_inc(pe.h, 1); pe.v += 1
                    on('pe', lav)
                    aw_guard[half] = pe.v
                    gset(('aw', half), pe, pe.v)
                    slab_last = pe.v
                lav_v = pe.v

                def lep(ve, tc=tc, lav_v=lav_v):
                    W(ve, 'dv', pe, lav_v)
                    ve.tensor_copy(lrow[0:1, tc * TC:(tc + 1) * TC],
                                   PS[5][0:1, :TC]).then_inc(dv.h, 1)
                    dv.v += 1
                on('dv', lep)
                lep_v = dv.v

                def lbc(te, tc=tc, lep_v=lep_v):
                    W(te, 'pe', dv, lep_v)
                    te.matmul(PS[5][:, :TC], ones1[:],
                              lrow[0:1, tc * TC:(tc + 1) * TC],
                              start=True, stop=True).then_inc(pe.h, 1)
                    pe.v += 1
                on('pe', lbc)
                lbc_v = pe.v

                def avev(ve, h=h, tc=tc, lbc_v=lbc_v):
                    cs, ce = tc * TC, (tc + 1) * TC
                    W(ve, 'dv', pe, lbc_v)
                    gwait(ve, 'dv', 'scrfq')
                    ve.reciprocal(scrf[:, 0:TC], PS[5][:, :TC]).then_inc(dv.h, 1)
                    dv.v += 1
                    W(ve, 'dv', dv, dv.v)
                    ve.tensor_mul(scrf[:, 512:512 + TC], PS[4][:, :TC],
                                  scrf[:, 0:TC]).then_inc(dv.h, 1)
                    dv.v += 1
                    W(ve, 'dv', dv, dv.v)
                    if (h, cs) in pre_w:
                        W(ve, 'dv', *pre_w[(h, cs)])
                    ve.scalar_tensor_tensor(pre[:, h, cs:ce], scrf[:, 512:512 + TC],
                                            ocol[:, 0:1], pre[:, h, cs:ce],
                                            ALU.mult, ALU.add).then_inc(dv.h, 1)
                    dv.v += 1
                    pre_w[(h, cs)] = (dv, dv.v)
                    gset('scrfq', dv, dv.v)
                on('dv', avev)
                gset(('ps', 4), dv, dv.v)
                gset(('ps', 5), dv, dv.v)
            slab_hist[h] = slab_last
            if h >= 1:
                gset(('slab', (h + 1) % 2), pe, slab_hist[h - 1])

        while not mem_done[0]:
            emit_one_memq()
        for bi in range(4):
            gset(('ps', bi), ac, ac.v)
        close_scope(esS)
        close_scope(esB_)
        close_scope(esQ)

        esX2 = ExitStack()
        X2 = esX2.enter_context(nc.sbuf_tensor("X2", [128, DC, T], F32, side="right"))
        esU2 = ExitStack()
        pb = esU2.enter_context(nc.sbuf_tensor("pb", [128, DC, T], BF16, side="right"))
        xrel = {}

        def xreload(sp):
            W(sp, 'sp', hsp, 16)
            half = DC // 2
            q4 = DC // 4
            for qi in range(2):
                sp.dma_start(out=X2[:, qi * q4:(qi + 1) * q4, :],
                             in_=xsp_r[:, qi * q4:(qi + 1) * q4, :]).then_inc(hx2[0].h, 16)
                hx2[0].v += 16
            for e in range(half):
                xrel[e] = (hx2[0], hx2[0].v)
            for qi in range(2, 4):
                sp.dma_start(out=X2[:, qi * q4:(qi + 1) * q4, :],
                             in_=xsp_r[:, qi * q4:(qi + 1) * q4, :]).then_inc(hx2[1].h, 16)
                hx2[1].v += 16
            for e in range(half, DC):
                xrel[e] = (hx2[1], hx2[1].v)
        on('sp', xreload)

        def prebf(ve):
            for c in range(DC):
                for tc in range(NTC):
                    cs, ce = tc * TC, (tc + 1) * TC
                    W(ve, 'dv', *pre_w[(c, cs)])
                    ve.tensor_copy(pb[:, c, cs:ce], pre[:, c, cs:ce]).then_inc(dv.h, 1)
                    dv.v += 1
        on('dv', prebf)
        prebf_v = dv.v

        SUPO = 256
        NSO = D // SUPO
        loadvo = {}

        def emit_load_o(os_):
            def body(gp, os_=os_):
                gwait(gp, 'pl', ('arena', os_ % 2))
                w_ = pw[os_ % 2]
                hc = DC // 2
                gp.dma_start(out=wa[os_ % 2][:, :hc * SUPO],
                             in_=wo_r[:, :hc, os_ * SUPO:(os_ + 1) * SUPO]
                             ).then_inc(w_.h, 16)
                gp.dma_start(out=wa[os_ % 2][:, hc * SUPO:DC * SUPO],
                             in_=wo_r[:, hc:, os_ * SUPO:(os_ + 1) * SUPO]
                             ).then_inc(w_.h, 16)
                w_.v += 32
                loadvo[os_] = (w_, w_.v)
            on('pl', body)
        emit_load_o(0)
        if NSO > 1:
            emit_load_o(1)
        for os_ in range(NSO):
            last_mm = 0
            for ko in range(SUPO // 128):
                k = os_ * (SUPO // 128) + ko
                for tc in range(NTC):
                    cs, ce = tc * TC, (tc + 1) * TC
                    slot = (k * NTC + tc) % 2

                    def mmo(te, os_=os_, ko=ko, cs=cs, ce=ce, slot=slot):
                        W(te, 'pe', *loadvo[os_])
                        W(te, 'pe', dv, prebf_v)
                        gwait(te, 'pe', ('ps', slot))
                        for c in range(DC):
                            ins = te.matmul(
                                PS[slot][:, :TC],
                                wa[os_ % 2][:, c * SUPO + ko * 128: c * SUPO + (ko + 1) * 128],
                                pb[:, c, cs:ce],
                                start=(c == 0), stop=(c == DC - 1))
                        ins.then_inc(pe.h, 1); pe.v += 1
                    on('pe', mmo)
                    mm_v = pe.v
                    last_mm = mm_v

                    def evo(ve, k=k, cs=cs, ce=ce, slot=slot, mm_v=mm_v):
                        W(ve, 'dv', pe, mm_v)
                        W(ve, 'dv', *xrel[k])
                        ve.tensor_add(X2[:, k, cs:ce], X2[:, k, cs:ce],
                                      PS[slot][:, :TC]).then_inc(dv.h, 1)
                        dv.v += 1
                    on('dv', evo)
                    gset(('ps', slot), dv, dv.v)
            gset(('arena', os_ % 2), pe, last_mm)
            if os_ + 2 < NSO:
                emit_load_o(os_ + 2)
        close_scope(esU2)
        close_scope(esP)
        guards.pop('uarena', None)
        gset('uarena', pe, pe.v)

        esU3 = ExitStack()
        u3_t = esU3.enter_context(nc.sbuf_tensor("ub", [128, DC, T], BF16))
        esM2 = ExitStack()
        masks2 = tuple(esM2.enter_context(nc.sbuf_tensor(n, [128, T], d))
                       for n, d in (("n0bf", BF16), ("n1bf", BF16),
                                    ("n0f", F32), ("n1f", F32)))
        u3_v = emit_ln(X2, masks2, (dv, dv.v), u3_t)
        ffn2_v, xadd2 = emit_ffn(X2, u3_v, masks2, "f2", u3_t)
        close_scope(esM2)
        close_scope(esU3)

        eq = DC // 4

        def store(sp):
            for q in range(4):
                need = max(xadd2[(e, cs)] for e in range(q * eq, (q + 1) * eq)
                           for cs in (tc * TC for tc in range(NTC)))
                W(sp, 'sp', dv, need)
                sp.dma_start(out=out_r[:, q * eq:(q + 1) * eq, :],
                             in_=X2[:, q * eq:(q + 1) * eq, :]).then_inc(hout.h, 16)
                hout.v += 16
            sp.wait_ge(hout.h, hout.v)
        on('sp', store)
        esX2.close()
    return nc


def _host_amask(sub, cfg):
    NB, NOB = cfg.NB, cfg.NOB
    OB0, OB1 = own_blocks(0, NB), own_blocks(1, NB)
    myOB = OB0 if sub == 0 else OB1
    am = np.ones((128, NB, 256), np.float32)
    for j in range(NB):
        us = min(bisect_left(OB0, j), bisect_left(OB1, j))
        mys = bisect_left(myOB, j)
        delta = mys - us
        if us >= NOB:
            continue
        am[:, j, :delta * 128] = 0.0
        if mys < NOB and myOB[mys] == j:
            tri = (np.arange(128)[:, None] <= np.arange(128)[None, :])
            am[:, j, delta * 128:(delta + 1) * 128] = tri.astype(np.float32)
    import ml_dtypes
    return am.astype(ml_dtypes.bfloat16)


def kernel(L0_attn_x, L0_attn_w_q, L0_attn_w_k, L0_attn_w_v, L0_attn_w_o,
           L0_attn_gate, L0_attn_memories, L0_attn_memory_norms,
           L0_ffn_Wg, L0_ffn_W0a, L0_ffn_b0a, L0_ffn_W0b, L0_ffn_b0b,
           L0_ffn_W1a, L0_ffn_b1a, L0_ffn_W1b, L0_ffn_b1b):
    cfg = CFG()
    D, F, NB, DC, FC, T = cfg.D, cfg.F, cfg.NB, cfg.DC, cfg.FC, cfg.T
    x = np.asarray(L0_attn_x, np.float32)
    ct = lambda a: np.ascontiguousarray(np.asarray(a, np.float32))
    shared = {
        "wqT": ct(np.asarray(L0_attn_w_q).T),
        "wkT": ct(np.asarray(L0_attn_w_k).T),
        "wvT": ct(np.asarray(L0_attn_w_v).T),
        "woT": ct(np.asarray(L0_attn_w_o).T),
        "W0aT": ct(np.asarray(L0_ffn_W0a).T),
        "W0bT": ct(np.asarray(L0_ffn_W0b).T),
        "W1aT": ct(np.asarray(L0_ffn_W1a).T),
        "W1bT": ct(np.asarray(L0_ffn_W1b).T),
        "WgT": ct(np.asarray(L0_ffn_Wg).T),
        "mems": ct(L0_attn_memories),
        "normsT": ct(np.asarray(L0_attn_memory_norms).T),
        "b0a2": ct(np.asarray(L0_ffn_b0a).reshape(FC, 128).T),
        "b1a2": ct(np.asarray(L0_ffn_b1a).reshape(FC, 128).T),
        "b0b2": ct(np.asarray(L0_ffn_b0b).reshape(DC, 128).T),
        "b1b2": ct(np.asarray(L0_ffn_b1b).reshape(DC, 128).T),
        "gate": ct(np.asarray(L0_attn_gate).reshape(1, 1)),
    }
    amasks = [_host_amask(0, cfg), _host_amask(1, cfg)]
    in_maps = []
    rows_per_core = []
    for c in range(8):
        b, sub = c // 2, c % 2
        ob = own_blocks(sub, NB)
        rows = np.concatenate([np.arange(g * 128, (g + 1) * 128) for g in ob])
        rows_per_core.append((b, rows))
        m = dict(shared)
        m["x"] = np.ascontiguousarray(x[b][rows].T)
        m["amask"] = amasks[sub]
        in_maps.append(m)

    nc = build(cfg)
    res = run_bass_kernel_spmd(nc, in_maps, core_ids=list(range(8)))
    out = np.empty_like(x)
    for c in range(8):
        b, rows = rows_per_core[c]
        out[b][rows] = res.results[c]["out"].T
    return out
